# version 04
# baseline (speedup 1.0000x reference)
import contextlib
import numpy as np
import ml_dtypes
import concourse.bass as bass
import concourse.mybir as mybir
from concourse.bass_utils import run_bass_kernel_spmd

F32 = mybir.dt.float32
BF16 = mybir.dt.bfloat16
I32 = mybir.dt.int32
ALU = mybir.AluOpType
ACTF = mybir.ActivationFunctionType
AXX = mybir.AxisListType.X

B, S, D = 4, 4096, 4096
H, HD, RD = 16, 256, 64
NDC = D // 128
NOC = D // 128
NTB = S // 128
NJ = 16
TS = 512
NSLAB = S // TS
NJS = 4
LN2C = float(np.log(2.0))
NUDGE = 1e-4
C2E = float((-3.0 + NUDGE - 0.5) * LN2C)
QG = 512

ENGS = ("sync", "tensor", "vector", "scalar", "gpsimd")


class Prog:

    def __init__(self, nc, es):
        self.nc, self.es = nc, es
        self.ops = {e: [] for e in ENGS}
        self.tr = {e: [] for e in ENGS}
        self.semobj, self.cnt = {}, {}
        self.nbar = 0

    def sem(self, name):
        self.semobj[name] = self.es.enter_context(self.nc.semaphore(name))
        self.cnt[name] = 0
        return name

    def wait(self, eng, sname, v):
        if v <= 0:
            return
        s = self.semobj[sname]
        self.ops[eng].append(lambda e, s=s, v=v: e.wait_ge(s, v))
        self.tr[eng].append(("w", sname, v))

    def dma(self, eng, sname, out, in_):
        self.cnt[sname] += 16
        s = self.semobj[sname]
        self.ops[eng].append(
            lambda e, o=out, i=in_, s=s: e.dma_start(out=o, in_=i).then_inc(s, 16))
        self.tr[eng].append(("i", sname, 16))
        return self.cnt[sname]

    def op(self, eng, fn):
        self.ops[eng].append(fn)

    def opi(self, eng, sname, fn):
        self.cnt[sname] += 1
        s = self.semobj[sname]

        def g(e, fn=fn, s=s):
            fn(e).then_inc(s, 1)
        self.ops[eng].append(g)
        self.tr[eng].append(("i", sname, 1))
        return self.cnt[sname]

    def barrier(self):
        self.nbar += 1
        for e in ENGS:
            self.opi(e, "s_ph", lambda en: en.nop())
        tgt = 5 * self.nbar
        for e in ENGS:
            self.wait(e, "s_ph", tgt)

    def check(self):
        for e in ENGS:
            for kind, sn, v in self.tr[e]:
                if kind == "w":
                    assert v <= self.cnt[sn], (
                        f"wait never satisfied: {e} waits {sn}>={v}, final {self.cnt[sn]}")
        pos = {e: 0 for e in ENGS}
        val = {sn: 0 for sn in self.cnt}
        progress = True
        while progress:
            progress = False
            for e in ENGS:
                while pos[e] < len(self.tr[e]):
                    kind, sn, v = self.tr[e][pos[e]]
                    if kind == "w" and val[sn] < v:
                        break
                    if kind == "i":
                        val[sn] += v
                    pos[e] += 1
                    progress = True
        stuck = {e: (pos[e], len(self.tr[e]), self.tr[e][pos[e]])
                 for e in ENGS if pos[e] < len(self.tr[e])}
        assert not stuck, f"deadlock in emitted program: {stuck}"


class Arena:
    def __init__(self, ap, cols):
        self.ap, self.cols = ap, cols
        self.off, self.peak = 0, 0

    def alloc(self, n):
        o = self.off
        self.off += n
        self.peak = max(self.peak, self.off)
        assert self.off <= self.cols, f"SBUF arena overflow: {self.off} > {self.cols}"
        return self.ap[:, o:o + n]

    def reset(self, keep):
        self.off = keep


def build_program(last_phase=4, debug=False):
    nc = bass.Bass("TRN2", target_bir_lowering=False)

    def scr_kind(ph):
        return "ExternalOutput" if (debug and last_phase == ph) else "Internal"

    xd = nc.dram_tensor("x", [S, D], F32, kind="ExternalInput").ap()
    wkd = wvd = wqd = wfd = wod = biasd = scald = roped = maskd = outd = None
    kT_s = v_s = qT_s = gT_s = aoT_s = None
    if last_phase >= 1:
        wkd = nc.dram_tensor("wk_t", [NOC, NDC, 128, 128], F32, kind="ExternalInput").ap()
        wvd = nc.dram_tensor("wv_t", [NOC, NDC, 128, 128], F32, kind="ExternalInput").ap()
        roped = nc.dram_tensor("rope", [2, 64, S], F32, kind="ExternalInput").ap()
        kT_s = nc.dram_tensor("kT_s", [H, 2, 128, S], F32, kind=scr_kind(1)).ap()
        v_s = nc.dram_tensor("v_s", [H, NTB, 128, HD], F32, kind=scr_kind(1)).ap()
    if last_phase >= 2:
        wqd = nc.dram_tensor("wq_t", [NOC, NDC, 128, 128], F32, kind="ExternalInput").ap()
        wfd = nc.dram_tensor("wf_t", [NOC, NDC, 128, 128], BF16, kind="ExternalInput").ap()
        biasd = nc.dram_tensor("bias_t", [128, NOC], F32, kind="ExternalInput").ap()
        scald = nc.dram_tensor("scal", [128, 3], F32, kind="ExternalInput").ap()
        qT_s = nc.dram_tensor("qT_s", [H, 2, 128, NJ * 128], F32, kind=scr_kind(2)).ap()
        gT_s = nc.dram_tensor("gT_s", [NOC, NJ, 128, 128], F32, kind=scr_kind(2)).ap()
    if last_phase >= 3:
        maskd = nc.dram_tensor("maskp", [128, 256], F32, kind="ExternalInput").ap()
        aoT_s = nc.dram_tensor("aoT_s", [NDC, NJ, 128, 128], F32, kind=scr_kind(3)).ap()
    if last_phase >= 4:
        wod = nc.dram_tensor("wo_t", [NOC, NDC, 128, 128], F32, kind="ExternalInput").ap()
        outd = nc.dram_tensor("out", [NJ, 128, D], F32, kind="ExternalOutput").ap()
    identd = nc.inline_tensor(np.eye(128, dtype=np.float32), name="identc").ap()
    epsd = nc.inline_tensor(np.full((128, 1), 1e-5, np.float32), name="epsc").ap()
    zerod = nc.inline_tensor(np.zeros((128, 512), np.float32), name="zeroc").ap()
    xnT_s = nc.dram_tensor("xnT_s", [NDC, NTB, 128, 128], F32, kind=scr_kind(0)).ap()
    lnsc_s = nc.dram_tensor("lnsc_s", [NTB, 128, 4], F32, kind=scr_kind(0)).ap() \
        if (debug and last_phase == 0) else None

    es = contextlib.ExitStack()
    es.__enter__()

    AF_COLS = 38912
    ab_f = es.enter_context(nc.sbuf_tensor("arena_f", [128, AF_COLS], F32))
    ab_b = es.enter_context(nc.sbuf_tensor("arena_b", [128, 24576], BF16))
    ab_i = es.enter_context(nc.sbuf_tensor("arena_i", [128, QG], I32))
    A = Arena(ab_f[:, :], AF_COLS)

    ident = A.alloc(128)
    maskb = A.alloc(256)
    biasb = A.alloc(NOC)
    scalb = A.alloc(3)
    derb = A.alloc(3)
    epsb = A.alloc(1)
    junkb = A.alloc(512)
    PERSIST = A.off

    def dve_sep(e):
        e.tensor_scalar(out=junkb[:, :], in0=junkb[:, :], scalar1=1.0,
                        scalar2=None, op0=ALU.mult, op1=ALU.bypass)

    psA = es.enter_context(nc.psum_tensor("psA", [128, 4096], F32))

    P = Prog(nc, es)
    for nm in ("s_ph", "s_mm", "s_ev", "s_dv", "s_ms",
               "s_xt0", "s_xt1", "s_xo0", "s_xo1",
               "s_rh", "s_rp",
               "s_wp0", "s_wp1", "s_wp2", "s_wf0", "s_wf1",
               "s_kb0", "s_kb1", "s_vb0", "s_vb1", "s_gb0", "s_gb1",
               "s_hd", "s_ao0", "s_ao1", "s_gl0", "s_gl1", "s_xr", "s_out"):
        P.sem(nm)
    s_wp = ("s_wp0", "s_wp1", "s_wp2")
    s_wf = ("s_wf0", "s_wf1")
    s_kb = ("s_kb0", "s_kb1")
    s_vb = ("s_vb0", "s_vb1")
    s_gb = ("s_gb0", "s_gb1")
    s_ao = ("s_ao0", "s_ao1")
    s_gl = ("s_gl0", "s_gl1")
    s_xt = ("s_xt0", "s_xt1")
    s_xo = ("s_xo0", "s_xo1")

    P.dma("sync", "s_ms", junkb[:, :], zerod[:, :])
    P.dma("sync", "s_ms", ident[:, :], identd[:, :])
    P.dma("sync", "s_ms", epsb[:, :], epsd[:, :])
    if last_phase >= 3:
        P.dma("sync", "s_ms", maskb[:, :], maskd[:, :])
    if last_phase >= 2:
        P.dma("sync", "s_ms", biasb[:, :], biasd[:, :])
        P.dma("sync", "s_ms", scalb[:, :], scald[:, :])
    MS = P.cnt["s_ms"]
    P.wait("scalar", "s_ms", MS)
    P.wait("vector", "s_ms", MS)
    P.wait("tensor", "s_ms", MS)
    if last_phase >= 2:
        P.op("vector", lambda e: e.reciprocal(derb[:, 0:1], scalb[:, 0:1]))
        P.op("vector", lambda e: e.reciprocal(derb[:, 2:3], scalb[:, 2:3]))
        P.opi("vector", "s_dv", lambda e: e.tensor_tensor(
            out=derb[:, 1:2], in0=scalb[:, 0:1], in1=scalb[:, 1:2], op=ALU.mult))

    A.reset(PERSIST)
    xtok = [A.alloc(D) for _ in range(2)]
    xn_t = [A.alloc(D) for _ in range(2)]
    ln_st = A.alloc(48)
    ln_mv = [A.alloc(2) for _ in range(2)]
    ln_sd = [A.alloc(1) for _ in range(2)]
    ln_rs = [A.alloc(1) for _ in range(2)]
    ln_nm = [A.alloc(1) for _ in range(2)]
    xnT4 = [A.alloc(512) for _ in range(2)]
    lnscb = A.alloc(4)

    mark_xn = {}
    mark_tr_end = {}
    mark_ev4 = {}
    xo_issued = [0, 0]

    for tb in range(NTB):
        bx = tb % 2
        if tb >= 2:
            P.wait("sync", "s_ev", mark_xn[tb - 2])
        P.dma("sync", s_xt[bx], xtok[bx][:, :], xd[tb * 128:(tb + 1) * 128, :])
        P.wait("vector", s_xt[bx], P.cnt[s_xt[bx]])

        def ln_stats(e, bx=bx):
            for blk in range(8):
                e.bn_stats(ln_st[:, blk * 6:(blk + 1) * 6],
                           xtok[bx][:, blk * 512:(blk + 1) * 512])
            dve_sep(e)
            return e.bn_aggr(ln_mv[bx][:, :], ln_st[:, :])
        dv1 = P.opi("vector", "s_dv", ln_stats)
        P.wait("scalar", "s_dv", dv1)
        ev1 = P.opi("scalar", "s_ev", lambda e, bx=bx: e.activation(
            out=ln_sd[bx][:, :], in_=ln_mv[bx][:, 1:2], func=ACTF.Sqrt,
            bias=epsb[:, :]))
        P.wait("vector", "s_ev", ev1)

        def ln_fin(e, bx=bx):
            e.reciprocal(ln_rs[bx][:, :], ln_sd[bx][:, :])
            dve_sep(e)
            return e.scalar_tensor_tensor(
                out=ln_nm[bx][:, :], in0=ln_mv[bx][:, 0:1], scalar=-1.0,
                in1=ln_rs[bx][:, :], op0=ALU.mult, op1=ALU.mult)
        dv2 = P.opi("vector", "s_dv", ln_fin)
        P.wait("scalar", "s_dv", dv2)
        if tb >= 2:
            P.wait("scalar", "s_mm", mark_tr_end[tb - 2])
        mark_xn[tb] = P.opi("scalar", "s_ev", lambda e, bx=bx: e.activation(
            out=xn_t[bx][:, :], in_=xtok[bx][:, :], func=ACTF.Identity,
            scale=ln_rs[bx][:, :], bias=ln_nm[bx][:, :]))
        if lnsc_s is not None:
            def dumpsc(e, bx=bx):
                e.copy(out=lnscb[:, 0:1], in_=ln_rs[bx][:, :])
                e.copy(out=lnscb[:, 1:2], in_=ln_nm[bx][:, :])
                return e.copy(out=lnscb[:, 2:4], in_=ln_mv[bx][:, :])
            dmk = P.opi("scalar", "s_ev", dumpsc)
            P.wait("gpsimd", "s_ev", dmk)
            P.dma("gpsimd", "s_ms", lnsc_s[tb], lnscb[:, 0:4])
            P.wait("scalar", "s_ms", P.cnt["s_ms"])

        P.wait("tensor", "s_ev", mark_xn[tb])
        for grp in range(8):
            pb = grp % 2
            prev = mark_ev4.get((tb, grp - 2)) if grp >= 2 else mark_ev4.get((tb - 1, grp + 6))
            if prev:
                P.wait("tensor", "s_ev", prev)
            mm = None
            for u in range(4):
                dc = grp * 4 + u
                if u == 3:
                    mm = P.opi("tensor", "s_mm", lambda e, bx=bx, dc=dc, pb=pb: e.transpose(
                        psA[:, pb * 512 + (dc % 4) * 128: pb * 512 + (dc % 4) * 128 + 128],
                        xn_t[bx][:, dc * 128:(dc + 1) * 128], ident[:, :]))
                else:
                    P.op("tensor", lambda e, bx=bx, dc=dc, pb=pb: e.transpose(
                        psA[:, pb * 512 + (dc % 4) * 128: pb * 512 + (dc % 4) * 128 + 128],
                        xn_t[bx][:, dc * 128:(dc + 1) * 128], ident[:, :]))
            if grp == 7:
                mark_tr_end[tb] = mm
            be = grp % 2
            P.wait("scalar", "s_mm", mm)
            P.wait("scalar", s_xo[be], xo_issued[be])
            ev4 = P.opi("scalar", "s_ev", lambda e, pb=pb, be=be: e.copy(
                out=xnT4[be][:, :], in_=psA[:, pb * 512: pb * 512 + 512]))
            mark_ev4[(tb, grp)] = ev4
            P.wait("gpsimd", "s_ev", ev4)
            xo_issued[be] = P.dma(
                "gpsimd", s_xo[be],
                xnT_s[grp * 4:(grp + 1) * 4, tb].rearrange("a p c -> p a c"),
                xnT4[be][:, :].rearrange("p (a c) -> p a c", c=128))

    P.wait("gpsimd", s_xo[0], xo_issued[0])
    P.wait("gpsimd", s_xo[1], xo_issued[1])
    P.barrier()
    if last_phase < 1:
        _finish(P, es, nc)
        return nc

    A.reset(PERSIST)
    rhs = A.alloc(NDC * TS)
    wpan = [A.alloc(NDC * 128) for _ in range(3)]
    kbn = [A.alloc(TS) for _ in range(2)]
    vb1 = [A.alloc(TS) for _ in range(2)]
    vTb = [A.alloc(TS) for _ in range(2)]
    ropeb = A.alloc(2 * TS)
    ropesc = A.alloc(TS)

    kpan = [0]
    kps = [0]
    mark_pan = {}
    mark_psfree = {}
    kb_issued = [0, 0]
    vb_issued = [0, 0]
    kkb = [0]
    kvb = [0]
    mark_vtr = {}
    mark_vev = {}
    rope_use = [0]

    def stream_panel(wdr, oc, bufs, sems, kctr, ring):
        k = kctr[0]
        b = k % ring
        prev = mark_pan.get(("pan", id(kctr), k - ring))
        if prev:
            P.wait("sync", "s_mm", prev)
        P.dma("sync", sems[b], bufs[b][:, :].rearrange("p (a c) -> p a c", c=128),
              wdr[oc].rearrange("a p c -> p a c"))
        kctr[0] += 1
        return k, b

    def proj_mms(wbuf, wsem_name, rhs_ap, Tc, extra_waits=()):
        kp = kps[0]
        pb = kp % 2
        P.wait("tensor", wsem_name, P.cnt[wsem_name])
        P.wait("tensor", "s_rh", P.cnt["s_rh"])
        for sn, v in extra_waits:
            P.wait("tensor", sn, v)
        prev = mark_psfree.get(kp - 2)
        if prev:
            P.wait("tensor", prev[0], prev[1])
        mm = None
        for dc in range(NDC):
            st, sp = dc == 0, dc == NDC - 1
            if sp:
                mm = P.opi("tensor", "s_mm", lambda e, dc=dc, pb=pb, wbuf=wbuf,
                           rhs_ap=rhs_ap, Tc=Tc: e.matmul(
                    psA[:, pb * 512: pb * 512 + Tc],
                    wbuf[:, dc * 128:(dc + 1) * 128],
                    rhs_ap[:, dc * TS: dc * TS + Tc],
                    start=False, stop=True))
            else:
                P.op("tensor", lambda e, dc=dc, pb=pb, wbuf=wbuf, rhs_ap=rhs_ap,
                     Tc=Tc, st=st: e.matmul(
                    psA[:, pb * 512: pb * 512 + Tc],
                    wbuf[:, dc * 128:(dc + 1) * 128],
                    rhs_ap[:, dc * TS: dc * TS + Tc],
                    start=st, stop=False))
        kps[0] += 1
        return kp, pb, mm

    def kq_evac_rope_store(mm, pb, oc, dst_ap, rpv):
        kb = kkb[0]
        bb = kb % 2
        kkb[0] += 1
        P.wait("scalar", "s_mm", mm)
        P.wait("scalar", s_kb[bb], kb_issued[bb])
        ev = P.opi("scalar", "s_ev", lambda e, pb=pb, bb=bb: e.copy(
            out=kbn[bb][:, :], in_=psA[:, pb * 512: pb * 512 + 512]))
        if oc % 2 == 0:
            P.wait("vector", "s_ev", ev)
            P.wait("vector", "s_rp", rpv)

            def rope(e, bb=bb):
                sw = [i ^ 1 for i in range(32)]
                e.stream_shuffle(ropesc[0:64, :], kbn[bb][0:64, :], sw)
                e.tensor_tensor(out=ropesc[0:64, :], in0=ropesc[0:64, :],
                                in1=ropeb[0:64, TS:2 * TS], op=ALU.mult)
                e.tensor_tensor(out=kbn[bb][0:64, :], in0=kbn[bb][0:64, :],
                                in1=ropeb[0:64, 0:TS], op=ALU.mult)
                return e.tensor_tensor(out=kbn[bb][0:64, :], in0=kbn[bb][0:64, :],
                                       in1=ropesc[0:64, :], op=ALU.add)
            dvv = P.opi("vector", "s_dv", rope)
            rope_use[0] = dvv
            P.wait("gpsimd", "s_dv", dvv)
            evfree = ("s_dv", dvv)
        else:
            P.wait("gpsimd", "s_ev", ev)
            evfree = ("s_ev", ev)
        kb_issued[bb] = P.dma("gpsimd", s_kb[bb], dst_ap, kbn[bb][:, :])
        return evfree

    rhs_free = []
    for sl in range(NSLAB):
        t0 = sl * TS
        for sn, v in rhs_free:
            P.wait("sync", sn, v)
        for dc in range(NDC):
            P.dma("sync", "s_rh",
                  rhs[:, dc * TS:(dc + 1) * TS].rearrange("p (a c) -> p a c", c=128),
                  xnT_s[dc, 4 * sl:4 * sl + 4].rearrange("a p c -> p a c"))
        P.wait("sync", "s_dv", rope_use[0])
        P.dma("sync", "s_rp", ropeb[0:64, 0:TS], roped[0, :, t0:t0 + TS])
        rpv = P.dma("sync", "s_rp", ropeb[0:64, TS:2 * TS], roped[1, :, t0:t0 + TS])

        for mat in ("k", "v"):
            wdr = wkd if mat == "k" else wvd
            for oc in range(NOC):
                k, b = stream_panel(wdr, oc, wpan, s_wp, kpan, 3)
                kp, pb, mm = proj_mms(wpan[b], s_wp[b], rhs, TS)
                mark_pan[("pan", id(kpan), k)] = mm
                if mat == "k":
                    mark_psfree[kp] = kq_evac_rope_store(
                        mm, pb, oc, kT_s[oc // 2, oc % 2, :, t0:t0 + TS], rpv)
                else:
                    kv = kvb[0]
                    bb = kv % 2
                    kvb[0] += 1
                    P.wait("scalar", "s_mm", mm)
                    prevtr = mark_vtr.get(kv - 2)
                    if prevtr:
                        P.wait("scalar", "s_mm", prevtr)
                    ev = P.opi("scalar", "s_ev", lambda e, pb=pb, bb=bb: e.copy(
                        out=vb1[bb][:, :], in_=psA[:, pb * 512: pb * 512 + 512]))
                    mark_psfree[kp] = ("s_ev", ev)
                    P.wait("tensor", "s_ev", ev)
                    prevvev = mark_vev.get(kv - 2)
                    if prevvev:
                        P.wait("tensor", "s_ev", prevvev)
                    tbk = 2 + (kv % 2)
                    trm = None
                    for u in range(4):
                        if u == 3:
                            trm = P.opi("tensor", "s_mm", lambda e, bb=bb, u=u, tbk=tbk:
                                        e.transpose(
                                psA[:, tbk * 512 + u * 128: tbk * 512 + u * 128 + 128],
                                vb1[bb][:, u * 128:(u + 1) * 128], ident[:, :]))
                        else:
                            P.op("tensor", lambda e, bb=bb, u=u, tbk=tbk: e.transpose(
                                psA[:, tbk * 512 + u * 128: tbk * 512 + u * 128 + 128],
                                vb1[bb][:, u * 128:(u + 1) * 128], ident[:, :]))
                    mark_vtr[kv] = trm
                    P.wait("scalar", "s_mm", trm)
                    P.wait("scalar", s_vb[bb], vb_issued[bb])
                    vev = P.opi("scalar", "s_ev", lambda e, bb=bb, tbk=tbk: e.copy(
                        out=vTb[bb][:, :], in_=psA[:, tbk * 512: tbk * 512 + 512]))
                    mark_vev[kv] = vev
                    P.wait("gpsimd", "s_ev", vev)
                    vb_issued[bb] = P.dma(
                        "gpsimd", s_vb[bb],
                        v_s[oc // 2, 4 * sl:4 * sl + 4, :,
                            (oc % 2) * 128:(oc % 2) * 128 + 128].rearrange("a p c -> p a c"),
                        vTb[bb][:, :].rearrange("p (a c) -> p a c", c=128))
        rhs_free = [("s_mm", mark_pan[("pan", id(kpan), kpan[0] - 1)])]

    P.wait("gpsimd", s_kb[0], kb_issued[0])
    P.wait("gpsimd", s_kb[1], kb_issued[1])
    P.wait("gpsimd", s_vb[0], vb_issued[0])
    P.wait("gpsimd", s_vb[1], vb_issued[1])
    P.barrier()
    if last_phase < 2:
        _finish(P, es, nc)
        return nc

    qin = ab_b[:, 0:NDC * TS]
    wfp = [ab_b[:, NDC * TS + i * NDC * 128: NDC * TS + (i + 1) * NDC * 128]
           for i in range(2)]
    q0 = A.alloc(QG)
    q1 = A.alloc(QG)
    q2 = A.alloc(QG)
    q3 = A.alloc(QG)
    q4 = A.alloc(QG)
    fbn = [A.alloc(TS) for _ in range(2)]
    qint = ab_i[:, 0:QG]

    kwf = [0]
    mark_wfpan = {}
    gb_issued = [0, 0]
    kfb = [0]

    def emit_quant(src_ap, w, dst_bf=None, dst_f32=None, rmul=None):
        P.op("vector", lambda e, src=src_ap, w=w, r=rmul: e.tensor_scalar(
            out=q0[:, 0:w], in0=src, scalar1=r, scalar2=448.0,
            op0=ALU.mult, op1=ALU.min))
        d = P.opi("vector", "s_dv", lambda e, w=w: e.tensor_scalar(
            out=q0[:, 0:w], in0=q0[:, 0:w], scalar1=-448.0, scalar2=None,
            op0=ALU.max, op1=ALU.bypass))
        P.wait("scalar", "s_dv", d)
        v = P.opi("scalar", "s_ev", lambda e, w=w: e.activation(
            out=q1[:, 0:w], in_=q0[:, 0:w], func=ACTF.Abs))
        P.wait("vector", "s_ev", v)
        d = P.opi("vector", "s_dv", lambda e, w=w: e.tensor_scalar(
            out=q1[:, 0:w], in0=q1[:, 0:w], scalar1=2.0 ** -6, scalar2=None,
            op0=ALU.max, op1=ALU.bypass))
        P.wait("scalar", "s_dv", d)
        v = P.opi("scalar", "s_ev", lambda e, w=w: e.activation(
            out=q1[:, 0:w], in_=q1[:, 0:w], func=ACTF.Ln))
        P.wait("vector", "s_ev", v)
        d = P.opi("vector", "s_dv", lambda e, w=w: e.tensor_scalar(
            out=q1[:, 0:w], in0=q1[:, 0:w], scalar1=C2E, scalar2=1.0 / LN2C,
            op0=ALU.add, op1=ALU.mult))
        P.wait("scalar", "s_dv", d)

        def sc2(e, w=w):
            e.copy(out=qint[:, 0:w], in_=q1[:, 0:w])
            e.copy(out=q1[:, 0:w], in_=qint[:, 0:w])
            e.activation(out=q2[:, 0:w], in_=q1[:, 0:w], func=ACTF.Exp, scale=LN2C)
            return e.activation(out=q3[:, 0:w], in_=q1[:, 0:w], func=ACTF.Exp, scale=-LN2C)
        v = P.opi("scalar", "s_ev", sc2)
        P.wait("vector", "s_ev", v)

        def v2(e, w=w):
            e.tensor_tensor(out=q4[:, 0:w], in0=q0[:, 0:w], in1=q3[:, 0:w], op=ALU.mult)
            return e.tensor_scalar(out=q1[:, 0:w], in0=q4[:, 0:w], scalar1=-0.5,
                                   scalar2=None, op0=ALU.add, op1=ALU.bypass)
        d = P.opi("vector", "s_dv", v2)
        P.wait("scalar", "s_dv", d)

        def sc3(e, w=w):
            e.copy(out=qint[:, 0:w], in_=q1[:, 0:w])
            return e.copy(out=q3[:, 0:w], in_=qint[:, 0:w])
        v = P.opi("scalar", "s_ev", sc3)
        P.wait("vector", "s_ev", v)

        def v3(e, w=w, dst_bf=dst_bf, dst_f32=dst_f32):
            e.tensor_tensor(out=q1[:, 0:w], in0=q4[:, 0:w], in1=q3[:, 0:w],
                            op=ALU.subtract)
            e.tensor_scalar(out=q1[:, 0:w], in0=q1[:, 0:w], scalar1=0.5, scalar2=None,
                            op0=ALU.is_ge, op1=ALU.bypass)
            e.tensor_tensor(out=q1[:, 0:w], in0=q3[:, 0:w], in1=q1[:, 0:w], op=ALU.add)
            if dst_f32 is not None:
                return e.tensor_tensor(out=dst_f32, in0=q1[:, 0:w], in1=q2[:, 0:w],
                                       op=ALU.mult)
            e.tensor_tensor(out=q4[:, 0:w], in0=q1[:, 0:w], in1=q2[:, 0:w], op=ALU.mult)
            return e.tensor_copy(dst_bf, q4[:, 0:w])
        d = P.opi("vector", "s_dv", v3)
        return d, v

    rhs_free = []
    qin_free = []
    mark_qin = 0
    for js in range(NJS):
        for sn, v in rhs_free:
            P.wait("sync", sn, v)
        for dc in range(NDC):
            for u in range(4):
                P.dma("sync", "s_rh", rhs[:, dc * TS + u * 128: dc * TS + (u + 1) * 128],
                      xnT_s[dc, 8 * js + 2 * u])
        rhN = P.cnt["s_rh"]
        P.wait("sync", "s_dv", rope_use[0])
        rpv = None
        for u in range(4):
            gblk = (8 * js + 2 * u) * 128
            P.dma("sync", "s_rp", ropeb[0:64, u * 128:(u + 1) * 128],
                  roped[0, :, gblk:gblk + 128])
            rpv = P.dma("sync", "s_rp", ropeb[0:64, TS + u * 128:TS + (u + 1) * 128],
                        roped[1, :, gblk:gblk + 128])

        for oc in range(NOC):
            k, b = stream_panel(wqd, oc, wpan, s_wp, kpan, 3)
            kp, pb, mm = proj_mms(wpan[b], s_wp[b], rhs, TS)
            mark_pan[("pan", id(kpan), k)] = mm
            mark_psfree[kp] = kq_evac_rope_store(
                mm, pb, oc, qT_s[oc // 2, oc % 2, :, js * TS:(js + 1) * TS], rpv)
        mark_q_mm = mark_pan[("pan", id(kpan), kpan[0] - 1)]

        for sn, v in qin_free:
            P.wait("vector", sn, v)
        P.wait("vector", "s_rh", rhN)
        dq = vq = 0
        for gq in range(NDC * TS // QG):
            dq, vq = emit_quant(rhs[:, gq * QG:(gq + 1) * QG], QG,
                                dst_bf=qin[:, gq * QG:(gq + 1) * QG],
                                rmul=derb[:, 0:1])
        mark_qin = dq
        rhs_free = [("s_mm", mark_q_mm), ("s_dv", dq)]

        for oc in range(NOC):
            kf = kwf[0]
            bf = kf % 2
            prevf = mark_wfpan.get(kf - 2)
            if prevf:
                P.wait("sync", "s_mm", prevf)
            P.dma("sync", s_wf[bf], wfp[bf][:, :].rearrange("p (a c) -> p a c", c=128),
                  wfd[oc].rearrange("a p c -> p a c"))
            kwf[0] += 1
            extra = [("s_dv", mark_qin)] if oc == 0 else []
            kp, pb, mm = proj_mms(wfp[bf], s_wf[bf], qin, TS, extra_waits=extra)
            mark_wfpan[kf] = mm
            kb = kfb[0]
            bb = kb % 2
            kfb[0] += 1
            P.wait("scalar", "s_mm", mm)
            P.wait("scalar", s_gb[bb], gb_issued[bb])
            ev = P.opi("scalar", "s_ev", lambda e, pb=pb, bb=bb, oc=oc: e.activation(
                out=fbn[bb][:, :], in_=psA[:, pb * 512: pb * 512 + 512],
                func=ACTF.Identity, scale=derb[:, 1:2], bias=biasb[:, oc:oc + 1]))
            mark_psfree[kp] = ("s_ev", ev)
            P.wait("vector", "s_ev", ev)
            d2, _ = emit_quant(fbn[bb][:, :], TS, dst_f32=fbn[bb][:, :],
                               rmul=derb[:, 2:3])
            P.wait("scalar", "s_dv", d2)
            svm = P.opi("scalar", "s_ev", lambda e, bb=bb: e.activation(
                out=fbn[bb][:, :], in_=fbn[bb][:, :], func=ACTF.Silu,
                scale=scalb[:, 2:3]))
            P.wait("gpsimd", "s_ev", svm)
            gb_issued[bb] = P.dma(
                "gpsimd", s_gb[bb],
                gT_s[oc, 4 * js:4 * js + 4].rearrange("a p c -> p a c"),
                fbn[bb][:, :].rearrange("p (a c) -> p a c", c=128))
        qin_free = [("s_mm", mark_wfpan[kwf[0] - 1])]

    P.wait("gpsimd", s_kb[0], kb_issued[0])
    P.wait("gpsimd", s_kb[1], kb_issued[1])
    P.wait("gpsimd", s_gb[0], gb_issued[0])
    P.wait("gpsimd", s_gb[1], gb_issued[1])
    P.barrier()
    if last_phase < 3:
        _finish(P, es, nc)
        return nc

    A.reset(PERSIST)
    kbuf = A.alloc(2 * S)
    vbuf = A.alloc(NTB * HD)
    qbuf = A.alloc(2 * NJ * 128)
    aw = A.alloc(S)
    awT = A.alloc(S)
    mx8 = A.alloc(8)
    rmax = A.alloc(1)
    nbias = A.alloc(1)
    rsum = A.alloc(1)
    rinv = A.alloc(1)
    ao = [A.alloc(HD) for _ in range(2)]
    aot = [A.alloc(HD) for _ in range(2)]

    mark_head_pe = 0
    mark_avev = 0
    mark_trev_last = 0
    mark_awtr_last = 0
    ktr = [0]
    mark_tr_evac = {}
    kao = [0]
    ao_issued = [0, 0]

    for h in range(H):
        if h:
            P.wait("sync", "s_mm", mark_head_pe)
        P.dma("sync", "s_hd", kbuf[:, :].rearrange("p (a c) -> p a c", c=S),
              kT_s[h].rearrange("a p c -> p a c"))
        P.dma("sync", "s_hd", vbuf[:, :].rearrange("p (a c) -> p a c", c=HD),
              v_s[h].rearrange("a p c -> p a c"))
        P.dma("sync", "s_hd", qbuf[:, :].rearrange("p (a c) -> p a c", c=NJ * 128),
              qT_s[h].rearrange("a p c -> p a c"))
        hdN = P.cnt["s_hd"]
        P.wait("tensor", "s_hd", hdN)

        for j in range(NJ):
            E = (2 * j + 2) * 128
            nkb = (E + 511) // 512
            if mark_avev:
                P.wait("tensor", "s_ev", mark_avev)
            if mark_trev_last:
                P.wait("tensor", "s_ev", mark_trev_last)
            mm_sc = None
            for kb in range(nkb):
                w = min(512, E - kb * 512)
                for hc in range(2):
                    st, sp = hc == 0, hc == 1
                    if sp and kb == nkb - 1:
                        mm_sc = P.opi("tensor", "s_mm", lambda e, kb=kb, w=w, hc=hc, j=j:
                                      e.matmul(
                            psA[:, kb * 512: kb * 512 + w],
                            qbuf[:, hc * 2048 + j * 128: hc * 2048 + (j + 1) * 128],
                            kbuf[:, hc * S + kb * 512: hc * S + kb * 512 + w],
                            start=False, stop=True))
                    else:
                        P.op("tensor", lambda e, kb=kb, w=w, hc=hc, j=j, st=st, sp=sp:
                             e.matmul(
                            psA[:, kb * 512: kb * 512 + w],
                            qbuf[:, hc * 2048 + j * 128: hc * 2048 + (j + 1) * 128],
                            kbuf[:, hc * S + kb * 512: hc * S + kb * 512 + w],
                            start=st, stop=sp))
            P.wait("vector", "s_mm", mm_sc)

            def maskmax(e, E=E, nkb=nkb):
                e.tensor_tensor(out=psA[:, E - 256:E], in0=psA[:, E - 256:E],
                                in1=maskb[:, :], op=ALU.add)
                for bk in range(nkb):
                    w = min(512, E - bk * 512)
                    e.reduce_max(mx8[:, bk:bk + 1], psA[:, bk * 512: bk * 512 + w],
                                 axis=AXX)
                dve_sep(e)
                e.reduce_max(rmax[:, :], mx8[:, 0:nkb], axis=AXX)
                dve_sep(e)
                return e.tensor_scalar(out=nbias[:, :], in0=rmax[:, :],
                                       scalar1=-1.0 / 16.0, scalar2=None,
                                       op0=ALU.mult, op1=ALU.bypass)
            dvm = P.opi("vector", "s_dv", maskmax)
            P.wait("scalar", "s_dv", dvm)
            if mark_awtr_last:
                P.wait("scalar", "s_mm", mark_awtr_last)
            expm = P.opi("scalar", "s_ev", lambda e, E=E: e.activation(
                out=aw[:, 0:E], in_=psA[:, 0:E], func=ACTF.Exp, scale=1.0 / 16.0,
                bias=nbias[:, :], accum_out=rsum[:, :]))
            P.wait("vector", "s_ev", expm)
            rinvm = P.opi("vector", "s_dv", lambda e: e.reciprocal(
                rinv[:, :], rsum[:, :]))
            P.wait("tensor", "s_ev", expm)
            nkc = 2 * j + 2
            for grp in range((nkc + 3) // 4):
                ng = min(4, nkc - grp * 4)
                kt = ktr[0]
                bnk = 2 + (kt % 2)
                prevte = mark_tr_evac.get(kt - 2)
                if prevte:
                    P.wait("tensor", "s_ev", prevte)
                trm = None
                for u in range(ng):
                    kc = grp * 4 + u
                    if u == ng - 1:
                        trm = P.opi("tensor", "s_mm", lambda e, kc=kc, u=u, bnk=bnk:
                                    e.transpose(
                            psA[:, bnk * 512 + u * 128: bnk * 512 + u * 128 + 128],
                            aw[:, kc * 128:(kc + 1) * 128], ident[:, :]))
                    else:
                        P.op("tensor", lambda e, kc=kc, u=u, bnk=bnk: e.transpose(
                            psA[:, bnk * 512 + u * 128: bnk * 512 + u * 128 + 128],
                            aw[:, kc * 128:(kc + 1) * 128], ident[:, :]))
                P.wait("scalar", "s_mm", trm)
                tev = P.opi("scalar", "s_ev", lambda e, grp=grp, ng=ng, bnk=bnk: e.copy(
                    out=awT[:, grp * 512: grp * 512 + ng * 128],
                    in_=psA[:, bnk * 512: bnk * 512 + ng * 128]))
                mark_tr_evac[kt] = tev
                ktr[0] += 1
            mark_awtr_last = trm
            mark_trev_last = mark_tr_evac[ktr[0] - 1]
            P.wait("tensor", "s_ev", mark_trev_last)
            mm_av = None
            for kc in range(nkc):
                st, sp = kc == 0, kc == nkc - 1
                if sp:
                    mm_av = P.opi("tensor", "s_mm", lambda e, kc=kc: e.matmul(
                        psA[:, 0:256], awT[:, kc * 128:(kc + 1) * 128],
                        vbuf[:, kc * HD:(kc + 1) * HD], start=False, stop=True))
                else:
                    P.op("tensor", lambda e, kc=kc, st=st: e.matmul(
                        psA[:, 0:256], awT[:, kc * 128:(kc + 1) * 128],
                        vbuf[:, kc * HD:(kc + 1) * HD], start=st, stop=False))
            ka = kao[0]
            bo = ka % 2
            kao[0] += 1
            P.wait("scalar", "s_mm", mm_av)
            P.wait("scalar", "s_dv", rinvm)
            P.wait("scalar", s_ao[bo], ao_issued[bo])
            avev = P.opi("scalar", "s_ev", lambda e, bo=bo: e.activation(
                out=ao[bo][:, :], in_=psA[:, 0:256], func=ACTF.Identity,
                scale=rinv[:, :]))
            mark_avev = avev
            P.wait("tensor", "s_ev", avev)
            kt = ktr[0]
            bnk = 2 + (kt % 2)
            prevte = mark_tr_evac.get(kt - 2)
            if prevte:
                P.wait("tensor", "s_ev", prevte)
            for u in range(2):
                if u == 1:
                    trm = P.opi("tensor", "s_mm", lambda e, bo=bo, u=u, bnk=bnk:
                                e.transpose(
                        psA[:, bnk * 512 + u * 128: bnk * 512 + u * 128 + 128],
                        ao[bo][:, u * 128:(u + 1) * 128], ident[:, :]))
                else:
                    P.op("tensor", lambda e, bo=bo, u=u, bnk=bnk: e.transpose(
                        psA[:, bnk * 512 + u * 128: bnk * 512 + u * 128 + 128],
                        ao[bo][:, u * 128:(u + 1) * 128], ident[:, :]))
            P.wait("scalar", "s_mm", trm)
            tev = P.opi("scalar", "s_ev", lambda e, bo=bo, bnk=bnk: e.copy(
                out=aot[bo][:, :], in_=psA[:, bnk * 512: bnk * 512 + 256]))
            mark_tr_evac[kt] = tev
            ktr[0] += 1
            mark_trev_last = tev
            P.wait("gpsimd", "s_ev", tev)
            P.dma("gpsimd", s_ao[bo], aoT_s[2 * h, j], aot[bo][:, 0:128])
            ao_issued[bo] = P.dma("gpsimd", s_ao[bo], aoT_s[2 * h + 1, j],
                                  aot[bo][:, 128:256])
        mark_head_pe = P.cnt["s_mm"]

    P.wait("gpsimd", s_ao[0], ao_issued[0])
    P.wait("gpsimd", s_ao[1], ao_issued[1])
    P.barrier()
    if last_phase < 4:
        _finish(P, es, nc)
        return nc

    A.reset(PERSIST)
    T4 = 256
    rhs4 = A.alloc(NDC * T4)
    wpan4 = [A.alloc(NDC * 128) for _ in range(3)]
    gtl = [A.alloc(T4) for _ in range(2)]
    sgb = [A.alloc(T4) for _ in range(2)]
    outasm = [A.alloc(D) for _ in range(2)]
    xrow = A.alloc(D)

    kpan4 = [0]
    kps4 = [0]
    mark_pan4 = {}
    mark_ps4 = {}
    kgl = [0]
    gl_issued = [0, 0]
    ksg = [0]
    mark_sgtr = {}
    ktr4 = [0]
    mark_tr_evac4 = {}
    mark_oa_ev = {}
    mark_oa_add = 0
    out_issued = 0
    rhs4_free = 0

    for hs in range(8):
        if rhs4_free:
            P.wait("sync", "s_mm", rhs4_free)
        for dc in range(NDC):
            P.dma("sync", "s_rh", rhs4[:, dc * T4:(dc + 1) * T4]
                  .rearrange("p (a c) -> p a c", c=128),
                  aoT_s[dc, 2 * hs:2 * hs + 2].rearrange("a p c -> p a c"))
        rhN = P.cnt["s_rh"]
        for oc in range(NOC):
            k4 = kpan4[0]
            b = k4 % 3
            prev = mark_pan4.get(k4 - 3)
            if prev:
                P.wait("sync", "s_mm", prev)
            P.dma("sync", s_wp[b], wpan4[b][:, :].rearrange("p (a c) -> p a c", c=128),
                  wod[oc].rearrange("a p c -> p a c"))
            kpan4[0] += 1
            kg = kgl[0]
            bg = kg % 2
            kgl[0] += 1
            prevg = gl_issued[bg]
            prevsg = mark_sgtr.get(("sg_dv", kg - 2))
            if prevsg:
                P.wait("sync", "s_dv", prevsg)
            glv = P.dma("sync", s_gl[bg], gtl[bg][:, :].rearrange("p (a c) -> p a c", c=128),
                        gT_s[oc, 2 * hs:2 * hs + 2].rearrange("a p c -> p a c"))
            kp = kps4[0]
            pb = kp % 2
            P.wait("tensor", s_wp[b], P.cnt[s_wp[b]])
            P.wait("tensor", "s_rh", rhN)
            prevps = mark_ps4.get(kp - 2)
            if prevps:
                P.wait("tensor", prevps[0], prevps[1])
            mm = None
            for dc in range(NDC):
                st, sp = dc == 0, dc == NDC - 1
                if sp:
                    mm = P.opi("tensor", "s_mm", lambda e, dc=dc, pb=pb, b=b: e.matmul(
                        psA[:, pb * 512: pb * 512 + T4],
                        wpan4[b][:, dc * 128:(dc + 1) * 128],
                        rhs4[:, dc * T4:(dc + 1) * T4], start=False, stop=True))
                else:
                    P.op("tensor", lambda e, dc=dc, pb=pb, b=b, st=st: e.matmul(
                        psA[:, pb * 512: pb * 512 + T4],
                        wpan4[b][:, dc * 128:(dc + 1) * 128],
                        rhs4[:, dc * T4:(dc + 1) * T4], start=st, stop=False))
            mark_pan4[k4] = mm
            kps4[0] += 1
            ks = ksg[0]
            bs = ks % 2
            ksg[0] += 1
            P.wait("vector", "s_mm", mm)
            P.wait("vector", s_gl[bg], glv)
            prevtr = mark_sgtr.get(ks - 2)
            if prevtr:
                P.wait("vector", "s_mm", prevtr)
            sgm = P.opi("vector", "s_dv", lambda e, pb=pb, bg=bg, bs=bs: e.tensor_tensor(
                out=sgb[bs][:, :], in0=psA[:, pb * 512: pb * 512 + T4],
                in1=gtl[bg][:, :], op=ALU.mult))
            mark_ps4[kp] = ("s_dv", sgm)
            mark_sgtr[("sg_dv", kg)] = sgm
            P.wait("tensor", "s_dv", sgm)
            kt = ktr4[0]
            bnk = 2 + (kt % 2)
            prevte = mark_tr_evac4.get(kt - 2)
            if prevte:
                P.wait("tensor", "s_ev", prevte)
            trm = None
            for u in range(2):
                if u == 1:
                    trm = P.opi("tensor", "s_mm", lambda e, bs=bs, u=u, bnk=bnk:
                                e.transpose(
                        psA[:, bnk * 512 + u * 128: bnk * 512 + u * 128 + 128],
                        sgb[bs][:, u * 128:(u + 1) * 128], ident[:, :]))
                else:
                    P.op("tensor", lambda e, bs=bs, u=u, bnk=bnk: e.transpose(
                        psA[:, bnk * 512 + u * 128: bnk * 512 + u * 128 + 128],
                        sgb[bs][:, u * 128:(u + 1) * 128], ident[:, :]))
            mark_sgtr[ks] = trm
            P.wait("scalar", "s_mm", trm)
            if oc == 0:
                P.wait("scalar", "s_out", out_issued)
                if mark_oa_add:
                    P.wait("scalar", "s_dv", mark_oa_add)
            tev = None
            for u in range(2):
                tev = P.opi("scalar", "s_ev", lambda e, u=u, oc=oc, bnk=bnk: e.copy(
                    out=outasm[u][:, oc * 128:(oc + 1) * 128],
                    in_=psA[:, bnk * 512 + u * 128: bnk * 512 + u * 128 + 128]))
            mark_tr_evac4[kt] = tev
            ktr4[0] += 1
            mark_oa_ev[(hs, 0)] = tev
            mark_oa_ev[(hs, 1)] = tev
        rhs4_free = mark_pan4[kpan4[0] - 1]
        for u in range(2):
            j = 2 * hs + u
            if mark_oa_add:
                P.wait("gpsimd", "s_dv", mark_oa_add)
            xv = P.dma("gpsimd", "s_xr", xrow[:, :], xd[2 * j * 128:(2 * j + 1) * 128, :])
            P.wait("vector", "s_xr", xv)
            P.wait("vector", "s_ev", mark_oa_ev[(hs, u)])
            mark_oa_add = P.opi("vector", "s_dv", lambda e, u=u: e.tensor_tensor(
                out=outasm[u][:, :], in0=outasm[u][:, :], in1=xrow[:, :], op=ALU.add))
            P.wait("gpsimd", "s_dv", mark_oa_add)
            out_issued = P.dma("gpsimd", "s_out", outd[j], outasm[u][:, :])

    P.wait("gpsimd", "s_out", out_issued)
    _finish(P, es, nc)
    return nc


def _finish(P, es, nc):
    P.check()
    block = es.enter_context(nc.Block())
    for name in ENGS:
        def body(eng, name=name):
            for f in P.ops[name]:
                f(eng)
        getattr(block, name)(body)
    es.close()


def _rope_tables():
    import jax
    import jax.numpy as jnp
    with jax.default_device(jax.devices("cpu")[0]):
        inv_freq = 1.0 / 10000.0 ** (jnp.arange(0, RD, 2, dtype=jnp.float32) / RD)
        ang = jnp.einsum("i,j->ij", jnp.arange(S, dtype=jnp.float32), inv_freq)
        sin = jnp.repeat(jnp.sin(ang), 2, axis=-1)
        cos = jnp.repeat(jnp.cos(ang), 2, axis=-1)
        sin = np.asarray(sin, dtype=np.float32)
        cos = np.asarray(cos, dtype=np.float32)
    sign = np.where(np.arange(RD) % 2 == 0, np.float32(-1.0), np.float32(1.0))
    sinT = np.ascontiguousarray(sin.T * sign[:, None])
    cosT = np.ascontiguousarray(cos.T)
    return np.stack([cosT, sinT]).astype(np.float32)


def _tile_w(w):
    return np.ascontiguousarray(
        w.reshape(NOC, 128, NDC, 128).transpose(0, 2, 3, 1))


def _perm_blocks(a, axis):
    sh = a.shape
    n = sh[axis] // 128
    idx = np.arange(n)
    idx = idx ^ 1
    moved = np.moveaxis(a, axis, 0)
    moved = moved.reshape(n, 128, *moved.shape[1:])[idx]
    moved = moved.reshape(n * 128, *moved.shape[2:])
    return np.ascontiguousarray(np.moveaxis(moved, 0, axis))


def _mask_tile(parity):
    tri = np.where(np.arange(128)[:, None] >= np.arange(128)[None, :],
                   np.float32(0.0), np.float32(-1e9))
    right = np.full((128, 128), -1e9 if parity == 0 else 0.0, np.float32)
    return np.ascontiguousarray(np.concatenate([tri, right], axis=1))


def _prep_in_maps(inputs, last_phase=4):
    x = np.ascontiguousarray(np.asarray(inputs["L0_attn_hidden_states"], np.float32))
    maps = []
    shared = {}
    if last_phase >= 1:
        shared["wk_t"] = _tile_w(np.asarray(inputs["L0_attn_wk"], np.float32))
        shared["wv_t"] = _tile_w(np.asarray(inputs["L0_attn_wv"], np.float32))
        rope = _rope_tables()
    if last_phase >= 2:
        shared["wq_t"] = _tile_w(np.asarray(inputs["L0_attn_wq"], np.float32))
        wf = np.asarray(inputs["L0_ffn_weight"], np.float32)
        wfb = wf.astype(ml_dtypes.bfloat16)
        assert np.array_equal(wfb.astype(np.float32), wf), \
            "ffn weight not exactly bf16-representable"
        shared["wf_t"] = _tile_w(wfb)
        shared["bias_t"] = np.ascontiguousarray(
            np.asarray(inputs["L0_ffn_bias"], np.float32).reshape(NOC, 128).T)
        scal = np.empty((128, 3), np.float32)
        scal[:, 0] = np.float32(np.asarray(inputs["L0_ffn_input_scale"]))
        scal[:, 1] = np.float32(np.asarray(inputs["L0_ffn_weight_scale"]))
        scal[:, 2] = np.float32(np.asarray(inputs["L0_ffn_output_scale"]))
        shared["scal"] = scal
    masks = [_mask_tile(0), _mask_tile(1)] if last_phase >= 3 else None
    for c in range(8):
        b, p = c // 2, c % 2
        m = dict(shared)
        xb = x[b]
        if p == 1:
            xb = _perm_blocks(xb, 0)
        m["x"] = xb
        if last_phase >= 1:
            m["rope"] = rope if p == 0 else _perm_blocks(rope, 2)
        if last_phase >= 3:
            m["maskp"] = masks[p]
        maps.append(m)
    return maps


def kernel(L0_attn_hidden_states, L0_attn_wq, L0_attn_wk, L0_attn_wv,
           L0_attn_wo, L0_ffn_weight, L0_ffn_weight_scale, L0_ffn_bias,
           L0_ffn_input_scale, L0_ffn_output_scale):
    inputs = dict(
        L0_attn_hidden_states=L0_attn_hidden_states, L0_attn_wq=L0_attn_wq,
        L0_attn_wk=L0_attn_wk, L0_attn_wv=L0_attn_wv, L0_attn_wo=L0_attn_wo,
        L0_ffn_weight=L0_ffn_weight, L0_ffn_weight_scale=L0_ffn_weight_scale,
        L0_ffn_bias=L0_ffn_bias, L0_ffn_input_scale=L0_ffn_input_scale,
        L0_ffn_output_scale=L0_ffn_output_scale)
    in_maps = _prep_in_maps(inputs, last_phase=4)
    wo_t = _tile_w(np.asarray(L0_attn_wo, np.float32))
    for m in in_maps:
        m["wo_t"] = wo_t
    nc = build_program(last_phase=4, debug=False)
    res = run_bass_kernel_spmd(nc, in_maps, core_ids=list(range(8)))
    out = np.empty((B, S, D), np.float32)
    for c in range(8):
        b, p = c // 2, c % 2
        r = np.asarray(res.results[c]["out"])
        for j in range(NJ):
            g = 2 * j + p
            out[b, g * 128:(g + 1) * 128, :] = r[j]
    return out


# version 06
# speedup vs baseline: 1.0952x; 1.0952x over previous
import contextlib
import numpy as np
import ml_dtypes
import concourse.bass as bass
import concourse.mybir as mybir
from concourse.bass_utils import run_bass_kernel_spmd

F32 = mybir.dt.float32
BF16 = mybir.dt.bfloat16
I32 = mybir.dt.int32
ALU = mybir.AluOpType
ACTF = mybir.ActivationFunctionType
AXX = mybir.AxisListType.X

B, S, D = 4, 4096, 4096
H, HD, RD = 16, 256, 64
NDC = D // 128
NOC = D // 128
NTB = S // 128
NJ = 16
TS = 512
NSLAB = S // TS
NJS = 4
LN2C = float(np.log(2.0))
NUDGE = 1e-4
C2E = float((-3.0 + NUDGE - 0.5) * LN2C)
QG = 512

ENGS = ("sync", "tensor", "vector", "scalar", "gpsimd")


class Prog:

    def __init__(self, nc, es):
        self.nc, self.es = nc, es
        self.ops = {e: [] for e in ENGS}
        self.tr = {e: [] for e in ENGS}
        self.semobj, self.cnt = {}, {}
        self.nbar = 0

    def sem(self, name):
        self.semobj[name] = self.es.enter_context(self.nc.semaphore(name))
        self.cnt[name] = 0
        return name

    def wait(self, eng, sname, v):
        if v <= 0:
            return
        s = self.semobj[sname]
        self.ops[eng].append(lambda e, s=s, v=v: e.wait_ge(s, v))
        self.tr[eng].append(("w", sname, v))

    def dma(self, eng, sname, out, in_):
        self.cnt[sname] += 16
        s = self.semobj[sname]
        self.ops[eng].append(
            lambda e, o=out, i=in_, s=s: e.dma_start(out=o, in_=i).then_inc(s, 16))
        self.tr[eng].append(("i", sname, 16))
        return self.cnt[sname]

    def op(self, eng, fn):
        self.ops[eng].append(fn)

    def opi(self, eng, sname, fn):
        self.cnt[sname] += 1
        s = self.semobj[sname]

        def g(e, fn=fn, s=s):
            fn(e).then_inc(s, 1)
        self.ops[eng].append(g)
        self.tr[eng].append(("i", sname, 1))
        return self.cnt[sname]

    def barrier(self):
        self.nbar += 1
        for e in ENGS:
            self.opi(e, "s_ph", lambda en: en.nop())
        tgt = 5 * self.nbar
        for e in ENGS:
            self.wait(e, "s_ph", tgt)

    def check(self):
        for e in ENGS:
            for kind, sn, v in self.tr[e]:
                if kind == "w":
                    assert v <= self.cnt[sn], (
                        f"wait never satisfied: {e} waits {sn}>={v}, final {self.cnt[sn]}")
        pos = {e: 0 for e in ENGS}
        val = {sn: 0 for sn in self.cnt}
        progress = True
        while progress:
            progress = False
            for e in ENGS:
                while pos[e] < len(self.tr[e]):
                    kind, sn, v = self.tr[e][pos[e]]
                    if kind == "w" and val[sn] < v:
                        break
                    if kind == "i":
                        val[sn] += v
                    pos[e] += 1
                    progress = True
        stuck = {e: (pos[e], len(self.tr[e]), self.tr[e][pos[e]])
                 for e in ENGS if pos[e] < len(self.tr[e])}
        assert not stuck, f"deadlock in emitted program: {stuck}"


class Arena:
    def __init__(self, ap, cols):
        self.ap, self.cols = ap, cols
        self.off, self.peak = 0, 0

    def alloc(self, n):
        o = self.off
        self.off += n
        self.peak = max(self.peak, self.off)
        assert self.off <= self.cols, f"SBUF arena overflow: {self.off} > {self.cols}"
        return self.ap[:, o:o + n]

    def reset(self, keep):
        self.off = keep


def build_program(last_phase=4, debug=False):
    nc = bass.Bass("TRN2", target_bir_lowering=False)

    def scr_kind(ph):
        return "ExternalOutput" if (debug and last_phase == ph) else "Internal"

    xd = nc.dram_tensor("x", [S, D], F32, kind="ExternalInput").ap()
    wkd = wvd = wqd = wfd = wod = biasd = scald = roped = maskd = outd = None
    kT_s = v_s = qT_s = gT_s = aoT_s = None
    if last_phase >= 1:
        wkd = nc.dram_tensor("wk_t", [NOC, NDC, 128, 128], F32, kind="ExternalInput").ap()
        wvd = nc.dram_tensor("wv_t", [NOC, NDC, 128, 128], F32, kind="ExternalInput").ap()
        roped = nc.dram_tensor("rope", [2, 64, S], F32, kind="ExternalInput").ap()
        kT_s = nc.dram_tensor("kT_s", [H, 2, 128, S], F32, kind=scr_kind(1)).ap()
        v_s = nc.dram_tensor("v_s", [H, NTB, 128, HD], F32, kind=scr_kind(1)).ap()
    if last_phase >= 2:
        wqd = nc.dram_tensor("wq_t", [NOC, NDC, 128, 128], F32, kind="ExternalInput").ap()
        wfd = nc.dram_tensor("wf_t", [NOC, NDC, 128, 128], BF16, kind="ExternalInput").ap()
        biasd = nc.dram_tensor("bias_t", [128, NOC], F32, kind="ExternalInput").ap()
        scald = nc.dram_tensor("scal", [128, 3], F32, kind="ExternalInput").ap()
        qT_s = nc.dram_tensor("qT_s", [H, 2, 128, NJ * 128], F32, kind=scr_kind(2)).ap()
        gT_s = nc.dram_tensor("gT_s", [NOC, NJ, 128, 128], F32, kind=scr_kind(2)).ap()
    if last_phase >= 3:
        maskd = nc.dram_tensor("maskp", [128, 256], F32, kind="ExternalInput").ap()
        aoT_s = nc.dram_tensor("aoT_s", [NDC, NJ, 128, 128], F32, kind=scr_kind(3)).ap()
    if last_phase >= 4:
        wod = nc.dram_tensor("wo_t", [NOC, NDC, 128, 128], F32, kind="ExternalInput").ap()
        outd = nc.dram_tensor("out", [NJ, 128, D], F32, kind="ExternalOutput").ap()
    identd = nc.inline_tensor(np.eye(128, dtype=np.float32), name="identc").ap()
    epsd = nc.inline_tensor(np.full((128, 1), 1e-5, np.float32), name="epsc").ap()
    zerod = nc.inline_tensor(np.zeros((128, 512), np.float32), name="zeroc").ap()
    xnT_s = nc.dram_tensor("xnT_s", [NDC, NTB, 128, 128], F32, kind=scr_kind(0)).ap()
    lnsc_s = nc.dram_tensor("lnsc_s", [NTB, 128, 4], F32, kind=scr_kind(0)).ap() \
        if (debug and last_phase == 0) else None

    es = contextlib.ExitStack()
    es.__enter__()

    AF_COLS = 38912
    ab_f = es.enter_context(nc.sbuf_tensor("arena_f", [128, AF_COLS], F32))
    ab_b = es.enter_context(nc.sbuf_tensor("arena_b", [128, 24576], BF16))
    ab_i = es.enter_context(nc.sbuf_tensor("arena_i", [128, QG], I32))
    A = Arena(ab_f[:, :], AF_COLS)

    ident = A.alloc(128)
    maskb = A.alloc(256)
    biasb = A.alloc(NOC)
    scalb = A.alloc(3)
    derb = A.alloc(3)
    epsb = A.alloc(1)
    junkb = A.alloc(512)
    PERSIST = A.off

    def dve_sep(e):
        e.tensor_scalar(out=junkb[:, :], in0=junkb[:, :], scalar1=1.0,
                        scalar2=None, op0=ALU.mult, op1=ALU.bypass)

    psA = es.enter_context(nc.psum_tensor("psA", [128, 4096], F32))

    P = Prog(nc, es)
    for nm in ("s_ph", "s_mm", "s_ev", "s_dv", "s_ms",
               "s_xt0", "s_xt1", "s_xo0", "s_xo1",
               "s_rh", "s_rp",
               "s_wp0", "s_wp1", "s_wp2", "s_wf0", "s_wf1",
               "s_kb0", "s_kb1", "s_vb0", "s_vb1", "s_gb0", "s_gb1",
               "s_hd", "s_ao0", "s_ao1", "s_gl0", "s_gl1", "s_xr", "s_out"):
        P.sem(nm)
    s_wp = ("s_wp0", "s_wp1", "s_wp2")
    s_wf = ("s_wf0", "s_wf1")
    s_kb = ("s_kb0", "s_kb1")
    s_vb = ("s_vb0", "s_vb1")
    s_gb = ("s_gb0", "s_gb1")
    s_ao = ("s_ao0", "s_ao1")
    s_gl = ("s_gl0", "s_gl1")
    s_xt = ("s_xt0", "s_xt1")
    s_xo = ("s_xo0", "s_xo1")

    P.dma("sync", "s_ms", junkb[:, :], zerod[:, :])
    P.dma("sync", "s_ms", ident[:, :], identd[:, :])
    P.dma("sync", "s_ms", epsb[:, :], epsd[:, :])
    if last_phase >= 3:
        P.dma("sync", "s_ms", maskb[:, :], maskd[:, :])
    if last_phase >= 2:
        P.dma("sync", "s_ms", biasb[:, :], biasd[:, :])
        P.dma("sync", "s_ms", scalb[:, :], scald[:, :])
    MS = P.cnt["s_ms"]
    P.wait("scalar", "s_ms", MS)
    P.wait("vector", "s_ms", MS)
    P.wait("tensor", "s_ms", MS)
    if last_phase >= 2:
        P.op("vector", lambda e: e.reciprocal(derb[:, 0:1], scalb[:, 0:1]))
        P.op("vector", lambda e: e.reciprocal(derb[:, 2:3], scalb[:, 2:3]))
        dvp = P.opi("vector", "s_dv", lambda e: e.tensor_tensor(
            out=derb[:, 1:2], in0=scalb[:, 0:1], in1=scalb[:, 1:2], op=ALU.mult))
        P.wait("vector", "s_dv", dvp)

    A.reset(PERSIST)
    xtok = [A.alloc(D) for _ in range(2)]
    xn_t = [A.alloc(D) for _ in range(2)]
    ln_st = A.alloc(48)
    ln_mv = [A.alloc(2) for _ in range(2)]
    ln_sd = [A.alloc(1) for _ in range(2)]
    ln_rs = [A.alloc(1) for _ in range(2)]
    ln_nm = [A.alloc(1) for _ in range(2)]
    xnT4 = [A.alloc(512) for _ in range(2)]
    lnscb = A.alloc(4)

    mark_xn = {}
    mark_tr_end = {}
    mark_ev4 = {}
    xo_issued = [0, 0]

    for tb in range(NTB):
        bx = tb % 2
        if tb >= 2:
            P.wait("sync", "s_ev", mark_xn[tb - 2])
        P.dma("sync", s_xt[bx], xtok[bx][:, :], xd[tb * 128:(tb + 1) * 128, :])
        P.wait("vector", s_xt[bx], P.cnt[s_xt[bx]])

        def ln_stats(e, bx=bx):
            for blk in range(7):
                e.bn_stats(ln_st[:, blk * 6:(blk + 1) * 6],
                           xtok[bx][:, blk * 512:(blk + 1) * 512])
            return e.bn_stats(ln_st[:, 42:48], xtok[bx][:, 3584:4096])
        dv0 = P.opi("vector", "s_dv", ln_stats)
        P.wait("vector", "s_dv", dv0)
        dv1 = P.opi("vector", "s_dv", lambda e, bx=bx: e.bn_aggr(
            ln_mv[bx][:, :], ln_st[:, :]))
        P.wait("scalar", "s_dv", dv1)
        ev1 = P.opi("scalar", "s_ev", lambda e, bx=bx: e.activation(
            out=ln_sd[bx][:, :], in_=ln_mv[bx][:, 1:2], func=ACTF.Sqrt,
            bias=epsb[:, :]))
        P.wait("vector", "s_ev", ev1)

        dvr = P.opi("vector", "s_dv", lambda e, bx=bx: e.reciprocal(
            ln_rs[bx][:, :], ln_sd[bx][:, :]))
        P.wait("vector", "s_dv", dvr)
        dv2 = P.opi("vector", "s_dv", lambda e, bx=bx: e.scalar_tensor_tensor(
            out=ln_nm[bx][:, :], in0=ln_mv[bx][:, 0:1], scalar=-1.0,
            in1=ln_rs[bx][:, :], op0=ALU.mult, op1=ALU.mult))
        P.wait("scalar", "s_dv", dv2)
        if tb >= 2:
            P.wait("scalar", "s_mm", mark_tr_end[tb - 2])
        mark_xn[tb] = P.opi("scalar", "s_ev", lambda e, bx=bx: e.activation(
            out=xn_t[bx][:, :], in_=xtok[bx][:, :], func=ACTF.Identity,
            scale=ln_rs[bx][:, :], bias=ln_nm[bx][:, :]))
        if lnsc_s is not None:
            def dumpsc(e, bx=bx):
                e.copy(out=lnscb[:, 0:1], in_=ln_rs[bx][:, :])
                e.copy(out=lnscb[:, 1:2], in_=ln_nm[bx][:, :])
                return e.copy(out=lnscb[:, 2:4], in_=ln_mv[bx][:, :])
            dmk = P.opi("scalar", "s_ev", dumpsc)
            P.wait("gpsimd", "s_ev", dmk)
            P.dma("gpsimd", "s_ms", lnsc_s[tb], lnscb[:, 0:4])
            P.wait("scalar", "s_ms", P.cnt["s_ms"])

        P.wait("tensor", "s_ev", mark_xn[tb])
        for grp in range(8):
            pb = grp % 2
            prev = mark_ev4.get((tb, grp - 2)) if grp >= 2 else mark_ev4.get((tb - 1, grp + 6))
            if prev:
                P.wait("tensor", "s_ev", prev)
            mm = None
            for u in range(4):
                dc = grp * 4 + u
                if u == 3:
                    mm = P.opi("tensor", "s_mm", lambda e, bx=bx, dc=dc, pb=pb: e.transpose(
                        psA[:, pb * 512 + (dc % 4) * 128: pb * 512 + (dc % 4) * 128 + 128],
                        xn_t[bx][:, dc * 128:(dc + 1) * 128], ident[:, :]))
                else:
                    P.op("tensor", lambda e, bx=bx, dc=dc, pb=pb: e.transpose(
                        psA[:, pb * 512 + (dc % 4) * 128: pb * 512 + (dc % 4) * 128 + 128],
                        xn_t[bx][:, dc * 128:(dc + 1) * 128], ident[:, :]))
            if grp == 7:
                mark_tr_end[tb] = mm
            be = grp % 2
            P.wait("scalar", "s_mm", mm)
            P.wait("scalar", s_xo[be], xo_issued[be])
            ev4 = P.opi("scalar", "s_ev", lambda e, pb=pb, be=be: e.copy(
                out=xnT4[be][:, :], in_=psA[:, pb * 512: pb * 512 + 512]))
            mark_ev4[(tb, grp)] = ev4
            P.wait("gpsimd", "s_ev", ev4)
            xo_issued[be] = P.dma(
                "gpsimd", s_xo[be],
                xnT_s[grp * 4:(grp + 1) * 4, tb].rearrange("a p c -> p a c"),
                xnT4[be][:, :].rearrange("p (a c) -> p a c", c=128))

    P.wait("gpsimd", s_xo[0], xo_issued[0])
    P.wait("gpsimd", s_xo[1], xo_issued[1])
    P.barrier()
    if last_phase < 1:
        _finish(P, es, nc)
        return nc

    A.reset(PERSIST)
    rhs = A.alloc(NDC * TS)
    wpan = [A.alloc(NDC * 128) for _ in range(3)]
    kbn = [A.alloc(TS) for _ in range(2)]
    vb1 = [A.alloc(TS) for _ in range(2)]
    vTb = [A.alloc(TS) for _ in range(2)]
    ropeb = A.alloc(2 * TS)
    ropesc = A.alloc(TS)

    kpan = [0]
    kps = [0]
    mark_pan = {}
    mark_psfree = {}
    kb_issued = [0, 0]
    vb_issued = [0, 0]
    kkb = [0]
    kvb = [0]
    mark_vtr = {}
    mark_vev = {}
    rope_use = [0]

    def stream_panel(wdr, oc, bufs, sems, kctr, ring):
        k = kctr[0]
        b = k % ring
        prev = mark_pan.get(("pan", id(kctr), k - ring))
        if prev:
            P.wait("sync", "s_mm", prev)
        P.dma("sync", sems[b], bufs[b][:, :].rearrange("p (a c) -> p a c", c=128),
              wdr[oc].rearrange("a p c -> p a c"))
        kctr[0] += 1
        return k, b

    def proj_mms(wbuf, wsem_name, rhs_ap, Tc, extra_waits=()):
        kp = kps[0]
        pb = kp % 2
        P.wait("tensor", wsem_name, P.cnt[wsem_name])
        P.wait("tensor", "s_rh", P.cnt["s_rh"])
        for sn, v in extra_waits:
            P.wait("tensor", sn, v)
        prev = mark_psfree.get(kp - 2)
        if prev:
            P.wait("tensor", prev[0], prev[1])
        mm = None
        for dc in range(NDC):
            st, sp = dc == 0, dc == NDC - 1
            if sp:
                mm = P.opi("tensor", "s_mm", lambda e, dc=dc, pb=pb, wbuf=wbuf,
                           rhs_ap=rhs_ap, Tc=Tc: e.matmul(
                    psA[:, pb * 512: pb * 512 + Tc],
                    wbuf[:, dc * 128:(dc + 1) * 128],
                    rhs_ap[:, dc * TS: dc * TS + Tc],
                    start=False, stop=True))
            else:
                P.op("tensor", lambda e, dc=dc, pb=pb, wbuf=wbuf, rhs_ap=rhs_ap,
                     Tc=Tc, st=st: e.matmul(
                    psA[:, pb * 512: pb * 512 + Tc],
                    wbuf[:, dc * 128:(dc + 1) * 128],
                    rhs_ap[:, dc * TS: dc * TS + Tc],
                    start=st, stop=False))
        kps[0] += 1
        return kp, pb, mm

    def kq_evac_rope_store(mm, pb, oc, dst_ap, rpv):
        kb = kkb[0]
        bb = kb % 2
        kkb[0] += 1
        P.wait("scalar", "s_mm", mm)
        P.wait("scalar", s_kb[bb], kb_issued[bb])
        ev = P.opi("scalar", "s_ev", lambda e, pb=pb, bb=bb: e.copy(
            out=kbn[bb][:, :], in_=psA[:, pb * 512: pb * 512 + 512]))
        if oc % 2 == 0:
            P.wait("vector", "s_ev", ev)
            P.wait("vector", "s_rp", rpv)
            sw = [i ^ 1 for i in range(32)]
            d1 = P.opi("vector", "s_dv", lambda e, bb=bb, sw=sw: e.stream_shuffle(
                ropesc[0:64, :], kbn[bb][0:64, :], sw))
            P.wait("vector", "s_dv", d1)

            def rope2(e, bb=bb):
                e.tensor_tensor(out=ropesc[0:64, :], in0=ropesc[0:64, :],
                                in1=ropeb[0:64, TS:2 * TS], op=ALU.mult)
                return e.tensor_tensor(out=kbn[bb][0:64, :], in0=kbn[bb][0:64, :],
                                       in1=ropeb[0:64, 0:TS], op=ALU.mult)
            d2 = P.opi("vector", "s_dv", rope2)
            P.wait("vector", "s_dv", d2)
            dvv = P.opi("vector", "s_dv", lambda e, bb=bb: e.tensor_tensor(
                out=kbn[bb][0:64, :], in0=kbn[bb][0:64, :],
                in1=ropesc[0:64, :], op=ALU.add))
            rope_use[0] = dvv
            P.wait("gpsimd", "s_dv", dvv)
            evfree = ("s_dv", dvv)
        else:
            P.wait("gpsimd", "s_ev", ev)
            evfree = ("s_ev", ev)
        kb_issued[bb] = P.dma("gpsimd", s_kb[bb], dst_ap, kbn[bb][:, :])
        return evfree

    rhs_free = []
    for sl in range(NSLAB):
        t0 = sl * TS
        for sn, v in rhs_free:
            P.wait("sync", sn, v)
        for dc in range(NDC):
            P.dma("sync", "s_rh",
                  rhs[:, dc * TS:(dc + 1) * TS].rearrange("p (a c) -> p a c", c=128),
                  xnT_s[dc, 4 * sl:4 * sl + 4].rearrange("a p c -> p a c"))
        P.wait("sync", "s_dv", rope_use[0])
        P.dma("sync", "s_rp", ropeb[0:64, 0:TS], roped[0, :, t0:t0 + TS])
        rpv = P.dma("sync", "s_rp", ropeb[0:64, TS:2 * TS], roped[1, :, t0:t0 + TS])

        for mat in ("k", "v"):
            wdr = wkd if mat == "k" else wvd
            for oc in range(NOC):
                k, b = stream_panel(wdr, oc, wpan, s_wp, kpan, 3)
                kp, pb, mm = proj_mms(wpan[b], s_wp[b], rhs, TS)
                mark_pan[("pan", id(kpan), k)] = mm
                if mat == "k":
                    mark_psfree[kp] = kq_evac_rope_store(
                        mm, pb, oc, kT_s[oc // 2, oc % 2, :, t0:t0 + TS], rpv)
                else:
                    kv = kvb[0]
                    bb = kv % 2
                    kvb[0] += 1
                    P.wait("scalar", "s_mm", mm)
                    prevtr = mark_vtr.get(kv - 2)
                    if prevtr:
                        P.wait("scalar", "s_mm", prevtr)
                    ev = P.opi("scalar", "s_ev", lambda e, pb=pb, bb=bb: e.copy(
                        out=vb1[bb][:, :], in_=psA[:, pb * 512: pb * 512 + 512]))
                    mark_psfree[kp] = ("s_ev", ev)
                    P.wait("tensor", "s_ev", ev)
                    prevvev = mark_vev.get(kv - 2)
                    if prevvev:
                        P.wait("tensor", "s_ev", prevvev)
                    tbk = 2 + (kv % 2)
                    trm = None
                    for u in range(4):
                        if u == 3:
                            trm = P.opi("tensor", "s_mm", lambda e, bb=bb, u=u, tbk=tbk:
                                        e.transpose(
                                psA[:, tbk * 512 + u * 128: tbk * 512 + u * 128 + 128],
                                vb1[bb][:, u * 128:(u + 1) * 128], ident[:, :]))
                        else:
                            P.op("tensor", lambda e, bb=bb, u=u, tbk=tbk: e.transpose(
                                psA[:, tbk * 512 + u * 128: tbk * 512 + u * 128 + 128],
                                vb1[bb][:, u * 128:(u + 1) * 128], ident[:, :]))
                    mark_vtr[kv] = trm
                    P.wait("scalar", "s_mm", trm)
                    P.wait("scalar", s_vb[bb], vb_issued[bb])
                    vev = P.opi("scalar", "s_ev", lambda e, bb=bb, tbk=tbk: e.copy(
                        out=vTb[bb][:, :], in_=psA[:, tbk * 512: tbk * 512 + 512]))
                    mark_vev[kv] = vev
                    P.wait("gpsimd", "s_ev", vev)
                    vb_issued[bb] = P.dma(
                        "gpsimd", s_vb[bb],
                        v_s[oc // 2, 4 * sl:4 * sl + 4, :,
                            (oc % 2) * 128:(oc % 2) * 128 + 128].rearrange("a p c -> p a c"),
                        vTb[bb][:, :].rearrange("p (a c) -> p a c", c=128))
        rhs_free = [("s_mm", mark_pan[("pan", id(kpan), kpan[0] - 1)])]

    P.wait("gpsimd", s_kb[0], kb_issued[0])
    P.wait("gpsimd", s_kb[1], kb_issued[1])
    P.wait("gpsimd", s_vb[0], vb_issued[0])
    P.wait("gpsimd", s_vb[1], vb_issued[1])
    P.barrier()
    if last_phase < 2:
        _finish(P, es, nc)
        return nc

    qin = ab_b[:, 0:NDC * TS]
    wfp = [ab_b[:, NDC * TS + i * NDC * 128: NDC * TS + (i + 1) * NDC * 128]
           for i in range(2)]
    q0 = A.alloc(QG)
    q1 = A.alloc(QG)
    q2 = A.alloc(QG)
    q3 = A.alloc(QG)
    q4 = A.alloc(QG)
    fbn = [A.alloc(TS) for _ in range(2)]
    qint = ab_i[:, 0:QG]

    kwf = [0]
    mark_wfpan = {}
    gb_issued = [0, 0]
    kfb = [0]

    def vop(fn):
        d = P.opi("vector", "s_dv", fn)
        P.wait("vector", "s_dv", d)
        return d

    def sop(fn):
        v = P.opi("scalar", "s_ev", fn)
        P.wait("scalar", "s_ev", v)
        return v

    def emit_quant(src_ap, w, dst_bf=None, dst_f32=None, rmul=None):
        vop(lambda e, src=src_ap, w=w, r=rmul: e.tensor_scalar(
            out=q0[:, 0:w], in0=src, scalar1=r, scalar2=448.0,
            op0=ALU.mult, op1=ALU.min))
        d = vop(lambda e, w=w: e.tensor_scalar(
            out=q0[:, 0:w], in0=q0[:, 0:w], scalar1=-448.0, scalar2=None,
            op0=ALU.max, op1=ALU.bypass))
        P.wait("scalar", "s_dv", d)
        v = sop(lambda e, w=w: e.activation(
            out=q1[:, 0:w], in_=q0[:, 0:w], func=ACTF.Abs))
        P.wait("vector", "s_ev", v)
        d = vop(lambda e, w=w: e.tensor_scalar(
            out=q1[:, 0:w], in0=q1[:, 0:w], scalar1=2.0 ** -6, scalar2=None,
            op0=ALU.max, op1=ALU.bypass))
        P.wait("scalar", "s_dv", d)
        v = sop(lambda e, w=w: e.activation(
            out=q1[:, 0:w], in_=q1[:, 0:w], func=ACTF.Ln))
        P.wait("vector", "s_ev", v)
        d = vop(lambda e, w=w: e.tensor_scalar(
            out=q1[:, 0:w], in0=q1[:, 0:w], scalar1=C2E, scalar2=1.0 / LN2C,
            op0=ALU.add, op1=ALU.mult))
        P.wait("scalar", "s_dv", d)
        sop(lambda e, w=w: e.copy(out=qint[:, 0:w], in_=q1[:, 0:w]))
        sop(lambda e, w=w: e.copy(out=q1[:, 0:w], in_=qint[:, 0:w]))

        def sc2b(e, w=w):
            e.activation(out=q2[:, 0:w], in_=q1[:, 0:w], func=ACTF.Exp, scale=LN2C)
            return e.activation(out=q3[:, 0:w], in_=q1[:, 0:w], func=ACTF.Exp,
                                scale=-LN2C)
        v = sop(sc2b)
        P.wait("vector", "s_ev", v)
        vop(lambda e, w=w: e.tensor_tensor(
            out=q4[:, 0:w], in0=q0[:, 0:w], in1=q3[:, 0:w], op=ALU.mult))
        d = vop(lambda e, w=w: e.tensor_scalar(
            out=q1[:, 0:w], in0=q4[:, 0:w], scalar1=-0.5, scalar2=None,
            op0=ALU.add, op1=ALU.bypass))
        P.wait("scalar", "s_dv", d)
        sop(lambda e, w=w: e.copy(out=qint[:, 0:w], in_=q1[:, 0:w]))
        v = sop(lambda e, w=w: e.copy(out=q3[:, 0:w], in_=qint[:, 0:w]))
        P.wait("vector", "s_ev", v)
        vop(lambda e, w=w: e.tensor_tensor(
            out=q1[:, 0:w], in0=q4[:, 0:w], in1=q3[:, 0:w], op=ALU.subtract))
        vop(lambda e, w=w: e.tensor_scalar(
            out=q1[:, 0:w], in0=q1[:, 0:w], scalar1=0.5, scalar2=None,
            op0=ALU.is_ge, op1=ALU.bypass))
        vop(lambda e, w=w: e.tensor_tensor(
            out=q1[:, 0:w], in0=q3[:, 0:w], in1=q1[:, 0:w], op=ALU.add))
        if dst_f32 is not None:
            d = vop(lambda e, w=w, dst_f32=dst_f32: e.tensor_tensor(
                out=dst_f32, in0=q1[:, 0:w], in1=q2[:, 0:w], op=ALU.mult))
            return d, v
        vop(lambda e, w=w: e.tensor_tensor(
            out=q4[:, 0:w], in0=q1[:, 0:w], in1=q2[:, 0:w], op=ALU.mult))
        d = vop(lambda e, w=w, dst_bf=dst_bf: e.tensor_copy(dst_bf, q4[:, 0:w]))
        return d, v

    rhs_free = []
    qin_free = []
    mark_qin = 0
    for js in range(NJS):
        for sn, v in rhs_free:
            P.wait("sync", sn, v)
        for dc in range(NDC):
            for u in range(4):
                P.dma("sync", "s_rh", rhs[:, dc * TS + u * 128: dc * TS + (u + 1) * 128],
                      xnT_s[dc, 8 * js + 2 * u])
        rhN = P.cnt["s_rh"]
        P.wait("sync", "s_dv", rope_use[0])
        rpv = None
        for u in range(4):
            gblk = (8 * js + 2 * u) * 128
            P.dma("sync", "s_rp", ropeb[0:64, u * 128:(u + 1) * 128],
                  roped[0, :, gblk:gblk + 128])
            rpv = P.dma("sync", "s_rp", ropeb[0:64, TS + u * 128:TS + (u + 1) * 128],
                        roped[1, :, gblk:gblk + 128])

        for oc in range(NOC):
            k, b = stream_panel(wqd, oc, wpan, s_wp, kpan, 3)
            kp, pb, mm = proj_mms(wpan[b], s_wp[b], rhs, TS)
            mark_pan[("pan", id(kpan), k)] = mm
            mark_psfree[kp] = kq_evac_rope_store(
                mm, pb, oc, qT_s[oc // 2, oc % 2, :, js * TS:(js + 1) * TS], rpv)
        mark_q_mm = mark_pan[("pan", id(kpan), kpan[0] - 1)]

        for sn, v in qin_free:
            P.wait("vector", sn, v)
        P.wait("vector", "s_rh", rhN)
        dq = vq = 0
        for gq in range(NDC * TS // QG):
            dq, vq = emit_quant(rhs[:, gq * QG:(gq + 1) * QG], QG,
                                dst_bf=qin[:, gq * QG:(gq + 1) * QG],
                                rmul=derb[:, 0:1])
        mark_qin = dq
        rhs_free = [("s_mm", mark_q_mm), ("s_dv", dq)]

        for oc in range(NOC):
            kf = kwf[0]
            bf = kf % 2
            prevf = mark_wfpan.get(kf - 2)
            if prevf:
                P.wait("sync", "s_mm", prevf)
            P.dma("sync", s_wf[bf], wfp[bf][:, :].rearrange("p (a c) -> p a c", c=128),
                  wfd[oc].rearrange("a p c -> p a c"))
            kwf[0] += 1
            extra = [("s_dv", mark_qin)] if oc == 0 else []
            kp, pb, mm = proj_mms(wfp[bf], s_wf[bf], qin, TS, extra_waits=extra)
            mark_wfpan[kf] = mm
            kb = kfb[0]
            bb = kb % 2
            kfb[0] += 1
            P.wait("scalar", "s_mm", mm)
            P.wait("scalar", s_gb[bb], gb_issued[bb])
            ev = P.opi("scalar", "s_ev", lambda e, pb=pb, bb=bb, oc=oc: e.activation(
                out=fbn[bb][:, :], in_=psA[:, pb * 512: pb * 512 + 512],
                func=ACTF.Identity, scale=derb[:, 1:2], bias=biasb[:, oc:oc + 1]))
            mark_psfree[kp] = ("s_ev", ev)
            P.wait("vector", "s_ev", ev)
            d2, _ = emit_quant(fbn[bb][:, :], TS, dst_f32=fbn[bb][:, :],
                               rmul=derb[:, 2:3])
            P.wait("scalar", "s_dv", d2)
            svm = P.opi("scalar", "s_ev", lambda e, bb=bb: e.activation(
                out=fbn[bb][:, :], in_=fbn[bb][:, :], func=ACTF.Silu,
                scale=scalb[:, 2:3]))
            P.wait("gpsimd", "s_ev", svm)
            gb_issued[bb] = P.dma(
                "gpsimd", s_gb[bb],
                gT_s[oc, 4 * js:4 * js + 4].rearrange("a p c -> p a c"),
                fbn[bb][:, :].rearrange("p (a c) -> p a c", c=128))
        qin_free = [("s_mm", mark_wfpan[kwf[0] - 1])]

    P.wait("gpsimd", s_kb[0], kb_issued[0])
    P.wait("gpsimd", s_kb[1], kb_issued[1])
    P.wait("gpsimd", s_gb[0], gb_issued[0])
    P.wait("gpsimd", s_gb[1], gb_issued[1])
    P.barrier()
    if last_phase < 3:
        _finish(P, es, nc)
        return nc

    A.reset(PERSIST)
    kbuf = A.alloc(2 * S)
    vbuf = A.alloc(NTB * HD)
    qbuf = A.alloc(2 * NJ * 128)
    aw = A.alloc(S)
    awT = A.alloc(S)
    mx8 = A.alloc(8)
    rmax = A.alloc(1)
    nbias = A.alloc(1)
    rsum = A.alloc(1)
    rinv = A.alloc(1)
    ao = [A.alloc(HD) for _ in range(2)]
    aot = [A.alloc(HD) for _ in range(2)]

    mark_head_pe = 0
    mark_avev = 0
    mark_trev_last = 0
    mark_awtr_last = 0
    ktr = [0]
    mark_tr_evac = {}
    kao = [0]
    ao_issued = [0, 0]

    for h in range(H):
        if h:
            P.wait("sync", "s_mm", mark_head_pe)
        P.dma("sync", "s_hd", kbuf[:, :].rearrange("p (a c) -> p a c", c=S),
              kT_s[h].rearrange("a p c -> p a c"))
        P.dma("sync", "s_hd", vbuf[:, :].rearrange("p (a c) -> p a c", c=HD),
              v_s[h].rearrange("a p c -> p a c"))
        P.dma("sync", "s_hd", qbuf[:, :].rearrange("p (a c) -> p a c", c=NJ * 128),
              qT_s[h].rearrange("a p c -> p a c"))
        hdN = P.cnt["s_hd"]
        P.wait("tensor", "s_hd", hdN)

        for j in range(NJ):
            E = (2 * j + 2) * 128
            nkb = (E + 511) // 512
            if mark_avev:
                P.wait("tensor", "s_ev", mark_avev)
            if mark_trev_last:
                P.wait("tensor", "s_ev", mark_trev_last)
            mm_sc = None
            for kb in range(nkb):
                w = min(512, E - kb * 512)
                for hc in range(2):
                    st, sp = hc == 0, hc == 1
                    if sp and kb == nkb - 1:
                        mm_sc = P.opi("tensor", "s_mm", lambda e, kb=kb, w=w, hc=hc, j=j:
                                      e.matmul(
                            psA[:, kb * 512: kb * 512 + w],
                            qbuf[:, hc * 2048 + j * 128: hc * 2048 + (j + 1) * 128],
                            kbuf[:, hc * S + kb * 512: hc * S + kb * 512 + w],
                            start=False, stop=True))
                    else:
                        P.op("tensor", lambda e, kb=kb, w=w, hc=hc, j=j, st=st, sp=sp:
                             e.matmul(
                            psA[:, kb * 512: kb * 512 + w],
                            qbuf[:, hc * 2048 + j * 128: hc * 2048 + (j + 1) * 128],
                            kbuf[:, hc * S + kb * 512: hc * S + kb * 512 + w],
                            start=st, stop=sp))
            P.wait("vector", "s_mm", mm_sc)

            dvk = P.opi("vector", "s_dv", lambda e, E=E: e.tensor_tensor(
                out=psA[:, E - 256:E], in0=psA[:, E - 256:E],
                in1=maskb[:, :], op=ALU.add))
            P.wait("vector", "s_dv", dvk)

            def redmax(e, E=E, nkb=nkb):
                last = None
                for bk in range(nkb):
                    w = min(512, E - bk * 512)
                    last = e.reduce_max(mx8[:, bk:bk + 1],
                                        psA[:, bk * 512: bk * 512 + w], axis=AXX)
                return last
            dva = P.opi("vector", "s_dv", redmax)
            P.wait("vector", "s_dv", dva)
            dvb = P.opi("vector", "s_dv", lambda e, nkb=nkb: e.reduce_max(
                rmax[:, :], mx8[:, 0:nkb], axis=AXX))
            P.wait("vector", "s_dv", dvb)
            dvm = P.opi("vector", "s_dv", lambda e: e.tensor_scalar(
                out=nbias[:, :], in0=rmax[:, :], scalar1=-1.0 / 16.0,
                scalar2=None, op0=ALU.mult, op1=ALU.bypass))
            P.wait("scalar", "s_dv", dvm)
            if mark_awtr_last:
                P.wait("scalar", "s_mm", mark_awtr_last)
            expm = P.opi("scalar", "s_ev", lambda e, E=E: e.activation(
                out=aw[:, 0:E], in_=psA[:, 0:E], func=ACTF.Exp, scale=1.0 / 16.0,
                bias=nbias[:, :], accum_out=rsum[:, :]))
            P.wait("vector", "s_ev", expm)
            rinvm = P.opi("vector", "s_dv", lambda e: e.reciprocal(
                rinv[:, :], rsum[:, :]))
            P.wait("tensor", "s_ev", expm)
            nkc = 2 * j + 2
            for grp in range((nkc + 3) // 4):
                ng = min(4, nkc - grp * 4)
                kt = ktr[0]
                bnk = 2 + (kt % 2)
                prevte = mark_tr_evac.get(kt - 2)
                if prevte:
                    P.wait("tensor", "s_ev", prevte)
                trm = None
                for u in range(ng):
                    kc = grp * 4 + u
                    if u == ng - 1:
                        trm = P.opi("tensor", "s_mm", lambda e, kc=kc, u=u, bnk=bnk:
                                    e.transpose(
                            psA[:, bnk * 512 + u * 128: bnk * 512 + u * 128 + 128],
                            aw[:, kc * 128:(kc + 1) * 128], ident[:, :]))
                    else:
                        P.op("tensor", lambda e, kc=kc, u=u, bnk=bnk: e.transpose(
                            psA[:, bnk * 512 + u * 128: bnk * 512 + u * 128 + 128],
                            aw[:, kc * 128:(kc + 1) * 128], ident[:, :]))
                P.wait("scalar", "s_mm", trm)
                tev = P.opi("scalar", "s_ev", lambda e, grp=grp, ng=ng, bnk=bnk: e.copy(
                    out=awT[:, grp * 512: grp * 512 + ng * 128],
                    in_=psA[:, bnk * 512: bnk * 512 + ng * 128]))
                mark_tr_evac[kt] = tev
                ktr[0] += 1
            mark_awtr_last = trm
            mark_trev_last = mark_tr_evac[ktr[0] - 1]
            P.wait("tensor", "s_ev", mark_trev_last)
            mm_av = None
            for kc in range(nkc):
                st, sp = kc == 0, kc == nkc - 1
                if sp:
                    mm_av = P.opi("tensor", "s_mm", lambda e, kc=kc: e.matmul(
                        psA[:, 0:256], awT[:, kc * 128:(kc + 1) * 128],
                        vbuf[:, kc * HD:(kc + 1) * HD], start=False, stop=True))
                else:
                    P.op("tensor", lambda e, kc=kc, st=st: e.matmul(
                        psA[:, 0:256], awT[:, kc * 128:(kc + 1) * 128],
                        vbuf[:, kc * HD:(kc + 1) * HD], start=st, stop=False))
            ka = kao[0]
            bo = ka % 2
            kao[0] += 1
            P.wait("scalar", "s_mm", mm_av)
            P.wait("scalar", "s_dv", rinvm)
            P.wait("scalar", s_ao[bo], ao_issued[bo])
            avev = P.opi("scalar", "s_ev", lambda e, bo=bo: e.activation(
                out=ao[bo][:, :], in_=psA[:, 0:256], func=ACTF.Identity,
                scale=rinv[:, :]))
            mark_avev = avev
            P.wait("tensor", "s_ev", avev)
            kt = ktr[0]
            bnk = 2 + (kt % 2)
            prevte = mark_tr_evac.get(kt - 2)
            if prevte:
                P.wait("tensor", "s_ev", prevte)
            for u in range(2):
                if u == 1:
                    trm = P.opi("tensor", "s_mm", lambda e, bo=bo, u=u, bnk=bnk:
                                e.transpose(
                        psA[:, bnk * 512 + u * 128: bnk * 512 + u * 128 + 128],
                        ao[bo][:, u * 128:(u + 1) * 128], ident[:, :]))
                else:
                    P.op("tensor", lambda e, bo=bo, u=u, bnk=bnk: e.transpose(
                        psA[:, bnk * 512 + u * 128: bnk * 512 + u * 128 + 128],
                        ao[bo][:, u * 128:(u + 1) * 128], ident[:, :]))
            P.wait("scalar", "s_mm", trm)
            tev = P.opi("scalar", "s_ev", lambda e, bo=bo, bnk=bnk: e.copy(
                out=aot[bo][:, :], in_=psA[:, bnk * 512: bnk * 512 + 256]))
            mark_tr_evac[kt] = tev
            ktr[0] += 1
            mark_trev_last = tev
            P.wait("gpsimd", "s_ev", tev)
            P.dma("gpsimd", s_ao[bo], aoT_s[2 * h, j], aot[bo][:, 0:128])
            ao_issued[bo] = P.dma("gpsimd", s_ao[bo], aoT_s[2 * h + 1, j],
                                  aot[bo][:, 128:256])
        mark_head_pe = P.cnt["s_mm"]

    P.wait("gpsimd", s_ao[0], ao_issued[0])
    P.wait("gpsimd", s_ao[1], ao_issued[1])
    P.barrier()
    if last_phase < 4:
        _finish(P, es, nc)
        return nc

    A.reset(PERSIST)
    T4 = 256
    rhs4 = A.alloc(NDC * T4)
    wpan4 = [A.alloc(NDC * 128) for _ in range(3)]
    gtl = [A.alloc(T4) for _ in range(2)]
    sgb = [A.alloc(T4) for _ in range(2)]
    outasm = [A.alloc(D) for _ in range(2)]
    xrow = A.alloc(D)

    kpan4 = [0]
    kps4 = [0]
    mark_pan4 = {}
    mark_ps4 = {}
    kgl = [0]
    gl_issued = [0, 0]
    ksg = [0]
    mark_sgtr = {}
    ktr4 = [0]
    mark_tr_evac4 = {}
    mark_oa_ev = {}
    mark_oa_add = 0
    out_issued = 0
    rhs4_free = 0

    for hs in range(8):
        if rhs4_free:
            P.wait("sync", "s_mm", rhs4_free)
        for dc in range(NDC):
            P.dma("sync", "s_rh", rhs4[:, dc * T4:(dc + 1) * T4]
                  .rearrange("p (a c) -> p a c", c=128),
                  aoT_s[dc, 2 * hs:2 * hs + 2].rearrange("a p c -> p a c"))
        rhN = P.cnt["s_rh"]
        for oc in range(NOC):
            k4 = kpan4[0]
            b = k4 % 3
            prev = mark_pan4.get(k4 - 3)
            if prev:
                P.wait("sync", "s_mm", prev)
            P.dma("sync", s_wp[b], wpan4[b][:, :].rearrange("p (a c) -> p a c", c=128),
                  wod[oc].rearrange("a p c -> p a c"))
            kpan4[0] += 1
            kg = kgl[0]
            bg = kg % 2
            kgl[0] += 1
            prevg = gl_issued[bg]
            prevsg = mark_sgtr.get(("sg_dv", kg - 2))
            if prevsg:
                P.wait("sync", "s_dv", prevsg)
            glv = P.dma("sync", s_gl[bg], gtl[bg][:, :].rearrange("p (a c) -> p a c", c=128),
                        gT_s[oc, 2 * hs:2 * hs + 2].rearrange("a p c -> p a c"))
            kp = kps4[0]
            pb = kp % 2
            P.wait("tensor", s_wp[b], P.cnt[s_wp[b]])
            P.wait("tensor", "s_rh", rhN)
            prevps = mark_ps4.get(kp - 2)
            if prevps:
                P.wait("tensor", prevps[0], prevps[1])
            mm = None
            for dc in range(NDC):
                st, sp = dc == 0, dc == NDC - 1
                if sp:
                    mm = P.opi("tensor", "s_mm", lambda e, dc=dc, pb=pb, b=b: e.matmul(
                        psA[:, pb * 512: pb * 512 + T4],
                        wpan4[b][:, dc * 128:(dc + 1) * 128],
                        rhs4[:, dc * T4:(dc + 1) * T4], start=False, stop=True))
                else:
                    P.op("tensor", lambda e, dc=dc, pb=pb, b=b, st=st: e.matmul(
                        psA[:, pb * 512: pb * 512 + T4],
                        wpan4[b][:, dc * 128:(dc + 1) * 128],
                        rhs4[:, dc * T4:(dc + 1) * T4], start=st, stop=False))
            mark_pan4[k4] = mm
            kps4[0] += 1
            ks = ksg[0]
            bs = ks % 2
            ksg[0] += 1
            P.wait("vector", "s_mm", mm)
            P.wait("vector", s_gl[bg], glv)
            prevtr = mark_sgtr.get(ks - 2)
            if prevtr:
                P.wait("vector", "s_mm", prevtr)
            sgm = P.opi("vector", "s_dv", lambda e, pb=pb, bg=bg, bs=bs: e.tensor_tensor(
                out=sgb[bs][:, :], in0=psA[:, pb * 512: pb * 512 + T4],
                in1=gtl[bg][:, :], op=ALU.mult))
            mark_ps4[kp] = ("s_dv", sgm)
            mark_sgtr[("sg_dv", kg)] = sgm
            P.wait("tensor", "s_dv", sgm)
            kt = ktr4[0]
            bnk = 2 + (kt % 2)
            prevte = mark_tr_evac4.get(kt - 2)
            if prevte:
                P.wait("tensor", "s_ev", prevte)
            trm = None
            for u in range(2):
                if u == 1:
                    trm = P.opi("tensor", "s_mm", lambda e, bs=bs, u=u, bnk=bnk:
                                e.transpose(
                        psA[:, bnk * 512 + u * 128: bnk * 512 + u * 128 + 128],
                        sgb[bs][:, u * 128:(u + 1) * 128], ident[:, :]))
                else:
                    P.op("tensor", lambda e, bs=bs, u=u, bnk=bnk: e.transpose(
                        psA[:, bnk * 512 + u * 128: bnk * 512 + u * 128 + 128],
                        sgb[bs][:, u * 128:(u + 1) * 128], ident[:, :]))
            mark_sgtr[ks] = trm
            P.wait("scalar", "s_mm", trm)
            if oc == 0:
                P.wait("scalar", "s_out", out_issued)
                if mark_oa_add:
                    P.wait("scalar", "s_dv", mark_oa_add)
            tev = None
            for u in range(2):
                tev = P.opi("scalar", "s_ev", lambda e, u=u, oc=oc, bnk=bnk: e.copy(
                    out=outasm[u][:, oc * 128:(oc + 1) * 128],
                    in_=psA[:, bnk * 512 + u * 128: bnk * 512 + u * 128 + 128]))
            mark_tr_evac4[kt] = tev
            ktr4[0] += 1
            mark_oa_ev[(hs, 0)] = tev
            mark_oa_ev[(hs, 1)] = tev
        rhs4_free = mark_pan4[kpan4[0] - 1]
        for u in range(2):
            j = 2 * hs + u
            if mark_oa_add:
                P.wait("gpsimd", "s_dv", mark_oa_add)
            xv = P.dma("gpsimd", "s_xr", xrow[:, :], xd[2 * j * 128:(2 * j + 1) * 128, :])
            P.wait("vector", "s_xr", xv)
            P.wait("vector", "s_ev", mark_oa_ev[(hs, u)])
            mark_oa_add = P.opi("vector", "s_dv", lambda e, u=u: e.tensor_tensor(
                out=outasm[u][:, :], in0=outasm[u][:, :], in1=xrow[:, :], op=ALU.add))
            P.wait("gpsimd", "s_dv", mark_oa_add)
            out_issued = P.dma("gpsimd", "s_out", outd[j], outasm[u][:, :])

    P.wait("gpsimd", "s_out", out_issued)
    _finish(P, es, nc)
    return nc


def _finish(P, es, nc):
    P.check()
    block = es.enter_context(nc.Block())
    for name in ENGS:
        def body(eng, name=name):
            for f in P.ops[name]:
                f(eng)
        getattr(block, name)(body)
    es.close()


def _rope_tables():
    import jax
    import jax.numpy as jnp
    with jax.default_device(jax.devices("cpu")[0]):
        inv_freq = 1.0 / 10000.0 ** (jnp.arange(0, RD, 2, dtype=jnp.float32) / RD)
        ang = jnp.einsum("i,j->ij", jnp.arange(S, dtype=jnp.float32), inv_freq)
        sin = jnp.repeat(jnp.sin(ang), 2, axis=-1)
        cos = jnp.repeat(jnp.cos(ang), 2, axis=-1)
        sin = np.asarray(sin, dtype=np.float32)
        cos = np.asarray(cos, dtype=np.float32)
    sign = np.where(np.arange(RD) % 2 == 0, np.float32(-1.0), np.float32(1.0))
    sinT = np.ascontiguousarray(sin.T * sign[:, None])
    cosT = np.ascontiguousarray(cos.T)
    return np.stack([cosT, sinT]).astype(np.float32)


def _tile_w(w):
    return np.ascontiguousarray(
        w.reshape(NOC, 128, NDC, 128).transpose(0, 2, 3, 1))


def _perm_blocks(a, axis):
    sh = a.shape
    n = sh[axis] // 128
    idx = np.arange(n)
    idx = idx ^ 1
    moved = np.moveaxis(a, axis, 0)
    moved = moved.reshape(n, 128, *moved.shape[1:])[idx]
    moved = moved.reshape(n * 128, *moved.shape[2:])
    return np.ascontiguousarray(np.moveaxis(moved, 0, axis))


def _mask_tile(parity):
    tri = np.where(np.arange(128)[:, None] >= np.arange(128)[None, :],
                   np.float32(0.0), np.float32(-1e9))
    right = np.full((128, 128), -1e9 if parity == 0 else 0.0, np.float32)
    return np.ascontiguousarray(np.concatenate([tri, right], axis=1))


def _prep_in_maps(inputs, last_phase=4):
    x = np.ascontiguousarray(np.asarray(inputs["L0_attn_hidden_states"], np.float32))
    maps = []
    shared = {}
    if last_phase >= 1:
        shared["wk_t"] = _tile_w(np.asarray(inputs["L0_attn_wk"], np.float32))
        shared["wv_t"] = _tile_w(np.asarray(inputs["L0_attn_wv"], np.float32))
        rope = _rope_tables()
    if last_phase >= 2:
        shared["wq_t"] = _tile_w(np.asarray(inputs["L0_attn_wq"], np.float32))
        wf = np.asarray(inputs["L0_ffn_weight"], np.float32)
        wfb = wf.astype(ml_dtypes.bfloat16)
        assert np.array_equal(wfb.astype(np.float32), wf), \
            "ffn weight not exactly bf16-representable"
        shared["wf_t"] = _tile_w(wfb)
        shared["bias_t"] = np.ascontiguousarray(
            np.asarray(inputs["L0_ffn_bias"], np.float32).reshape(NOC, 128).T)
        scal = np.empty((128, 3), np.float32)
        scal[:, 0] = np.float32(np.asarray(inputs["L0_ffn_input_scale"]))
        scal[:, 1] = np.float32(np.asarray(inputs["L0_ffn_weight_scale"]))
        scal[:, 2] = np.float32(np.asarray(inputs["L0_ffn_output_scale"]))
        shared["scal"] = scal
    masks = [_mask_tile(0), _mask_tile(1)] if last_phase >= 3 else None
    for c in range(8):
        b, p = c // 2, c % 2
        m = dict(shared)
        xb = x[b]
        if p == 1:
            xb = _perm_blocks(xb, 0)
        m["x"] = xb
        if last_phase >= 1:
            m["rope"] = rope if p == 0 else _perm_blocks(rope, 2)
        if last_phase >= 3:
            m["maskp"] = masks[p]
        maps.append(m)
    return maps


def kernel(L0_attn_hidden_states, L0_attn_wq, L0_attn_wk, L0_attn_wv,
           L0_attn_wo, L0_ffn_weight, L0_ffn_weight_scale, L0_ffn_bias,
           L0_ffn_input_scale, L0_ffn_output_scale):
    inputs = dict(
        L0_attn_hidden_states=L0_attn_hidden_states, L0_attn_wq=L0_attn_wq,
        L0_attn_wk=L0_attn_wk, L0_attn_wv=L0_attn_wv, L0_attn_wo=L0_attn_wo,
        L0_ffn_weight=L0_ffn_weight, L0_ffn_weight_scale=L0_ffn_weight_scale,
        L0_ffn_bias=L0_ffn_bias, L0_ffn_input_scale=L0_ffn_input_scale,
        L0_ffn_output_scale=L0_ffn_output_scale)
    in_maps = _prep_in_maps(inputs, last_phase=4)
    wo_t = _tile_w(np.asarray(L0_attn_wo, np.float32))
    for m in in_maps:
        m["wo_t"] = wo_t
    nc = build_program(last_phase=4, debug=False)
    res = run_bass_kernel_spmd(nc, in_maps, core_ids=list(range(8)))
    out = np.empty((B, S, D), np.float32)
    for c in range(8):
        b, p = c // 2, c % 2
        r = np.asarray(res.results[c]["out"])
        for j in range(NJ):
            g = 2 * j + p
            out[b, g * 128:(g + 1) * 128, :] = r[j]
    return out
